# version 21
# baseline (speedup 1.0000x reference)
import numpy as np
import ml_dtypes
import concourse.bass as bass
import concourse.mybir as mybir
from concourse.bass_utils import run_bass_kernel_spmd

F32 = mybir.dt.float32
BF16 = mybir.dt.bfloat16
AF = mybir.ActivationFunctionType
ALU = mybir.AluOpType
AX = mybir.AxisListType

ENGINES = ("sync", "tensor", "scalar", "vector", "gpsimd")
NEG = -1.0e9
TILT = 1e-6


class Plan:
    def __init__(self, nc):
        self.nc = nc
        self.items = {e: [] for e in ENGINES}
        self.sems = {}
        self.counts = {}

    def _sem(self, name):
        if name not in self.sems:
            self.sems[name] = self.nc.semaphore(name).__enter__()
            self.counts[name] = 0

    def op(self, engine, fn, sem=None, dma=False):
        it = ["op", fn, None, dma]
        self.items[engine].append(it)
        if sem is None:
            return None
        self._sem(sem)
        amt = 16 if dma else 1
        self.counts[sem] += amt
        it[2] = (sem, amt)
        return (sem, self.counts[sem])

    def nxt(self, sem):
        self._sem(sem)
        return (sem, self.counts[sem] + 1)

    def wait(self, engine, tok):
        if tok is None:
            return
        if isinstance(tok, (list, tuple)) and tok and isinstance(tok[0], (list, tuple)):
            for t in tok:
                self.wait(engine, t)
            return
        self.items[engine].append(["wait", tok[0], tok[1]])

    def check(self):
        from collections import deque
        ptr = {e: 0 for e in ENGINES}
        semv = {s: 0 for s in self.counts}
        queues = {e: deque() for e in ENGINES}
        progress = True
        while progress:
            progress = False
            for e in ENGINES:
                while ptr[e] < len(self.items[e]):
                    it = self.items[e][ptr[e]]
                    if it[0] == "wait":
                        if semv.get(it[1], 0) >= it[2]:
                            ptr[e] += 1
                            progress = True
                        else:
                            break
                    else:
                        if it[3]:
                            queues[e].append(it[2])
                        elif it[2] is not None:
                            semv[it[2][0]] += it[2][1]
                        ptr[e] += 1
                        progress = True
            for q in queues.values():
                while q:
                    inc = q.popleft()
                    if inc is not None:
                        semv[inc[0]] += inc[1]
                    progress = True
        stuck = {e: ptr[e] for e in ENGINES if ptr[e] < len(self.items[e])}
        if stuck:
            msg = []
            for e, pp in stuck.items():
                it = self.items[e][pp]
                msg.append(f"{e}@{pp}/{len(self.items[e])}: wait {it[1]}>={it[2]}"
                           f" (now {semv.get(it[1], 0)})")
            raise RuntimeError("Plan deadlock: " + "; ".join(msg))
        for s, v in self.counts.items():
            assert semv.get(s, 0) == v, (s, semv.get(s, 0), v)

    def emit(self):
        with self.nc.Block() as block:
            for ename in ENGINES:
                def make_body(items=self.items[ename], sems=self.sems):
                    def body(eng):
                        maxw = {}
                        for it in items:
                            if it[0] == "wait":
                                if maxw.get(it[1], -1) >= it[2]:
                                    continue
                                eng.wait_ge(sems[it[1]], it[2])
                                maxw[it[1]] = it[2]
                            else:
                                inst = it[1](eng)
                                if it[2] is not None:
                                    inst.then_inc(sems[it[2][0]], it[2][1])
                    return body
                getattr(block, ename)(make_body())


def build_nc(cfg):
    SEQ = cfg["seq"]
    TOWN = SEQ // 2
    HEADS = cfg["heads"]
    DM = 1024
    NCH = 8
    NT_OWN = TOWN // 128
    NT_ALL = SEQ // 128
    NQS = TOWN // 512
    NPAIR = HEADS // 2
    E = 32
    IMOE = 512
    NI = 4
    SHI = 1024
    NSH = 8
    TH = TOWN // 2
    NSL = TOWN // 512
    NSLH = TH // 512
    SCALE = 0.125

    nc = bass.Bass("TRN2", target_bir_lowering=False, debug=False)

    x_seq = nc.dram_tensor("x_seq", [SEQ, DM], F32, kind="ExternalInput").ap()
    wq_d = nc.dram_tensor("wq", [DM, 128 * NPAIR], F32, kind="ExternalInput").ap()
    wk_d = nc.dram_tensor("wk", [DM, 128 * NPAIR], F32, kind="ExternalInput").ap()
    wv_d = nc.dram_tensor("wv", [DM, 128 * NPAIR], F32, kind="ExternalInput").ap()
    wo_d = nc.dram_tensor("wo", [128 * NPAIR, DM], F32, kind="ExternalInput").ap()
    bo_d = nc.dram_tensor("bo", [1, DM], F32, kind="ExternalInput").ap()
    rw_d = nc.dram_tensor("rwT", [DM, E], F32, kind="ExternalInput").ap()
    rb_d = nc.dram_tensor("rbias", [1, E], F32, kind="ExternalInput").ap()
    gate_d = nc.dram_tensor("gateT", [E, DM, IMOE], F32, kind="ExternalInput").ap()
    up_d = nc.dram_tensor("upT", [E, DM, IMOE], F32, kind="ExternalInput").ap()
    down_d = nc.dram_tensor("downT", [E, IMOE, DM], F32, kind="ExternalInput").ap()
    sg_d = nc.dram_tensor("sgT", [DM, SHI], F32, kind="ExternalInput").ap()
    su_d = nc.dram_tensor("suT", [DM, SHI], F32, kind="ExternalInput").ap()
    sd_d = nc.dram_tensor("sdT", [SHI, DM], F32, kind="ExternalInput").ap()
    identb_d = nc.dram_tensor("ident_bf", [128, 128], BF16, kind="ExternalInput").ap()
    identf_d = nc.dram_tensor("ident_f", [128, 128], F32, kind="ExternalInput").ap()
    tri_d = nc.dram_tensor("tri_bf", [128, 128], BF16, kind="ExternalInput").ap()
    bando_d = nc.dram_tensor("band_other", [128, 128], BF16, kind="ExternalInput").ap()
    tilt_d = nc.dram_tensor("tilt", [128, E], F32, kind="ExternalInput").ap()
    sel_d = nc.dram_tensor("selmat", [E, E, 128], BF16, kind="ExternalInput").ap()
    ones_d = nc.dram_tensor("ones_f", [128, 128], F32, kind="ExternalInput").ap()
    bofm_d = nc.dram_tensor("bo_fm", [128, NCH], F32, kind="ExternalInput").ap()
    out_d = nc.dram_tensor("out", [TOWN, DM], F32, kind="ExternalOutput").ap()

    A = nc.alloc_sbuf_tensor
    xn_own = [A(f"xn_own{c}", [128, TOWN], BF16) for c in range(NCH)]
    o_acc = [A(f"o_acc{c}", [128, TOWN], BF16) for c in range(NCH)]
    s_identb = A("s_identb", [128, 128], BF16)
    s_identf = A("s_identf", [128, 128], F32)
    s_tri = A("s_tri", [128, 128], BF16)
    s_bando = A("s_bando", [128, 128], BF16)
    s_tilt = A("s_tilt", [128, E], F32)
    s_ones = A("s_ones", [128, 128], F32)
    s_onescol = A("s_onescol", [128, 1], BF16)
    s_onesb = A("s_onesb", [128, 64], BF16)
    s_bo_bc = A("s_bo_bc", [128, DM], F32)
    s_rb_bc = A("s_rb_bc", [128, E], F32)
    s_wfm = A("s_wfm", [E, TOWN], BF16)
    pbuf = [A(f"pbuf{i}", [128, 1024], BF16) for i in range(6)]
    x_ring = [A(f"x_ring{i}", [128, DM], F32) for i in range(2)]
    xn_tm = [A(f"xn_tm{i}", [128, DM], BF16) for i in range(2)]
    s_sum = A("s_sum", [128, NT_ALL], F32)
    s_sumsq = A("s_sumsq", [128, NT_ALL], F32)
    s_mean = A("s_mean", [128, NT_ALL], F32)
    s_rstd = A("s_rstd", [128, NT_ALL], F32)
    s_m2 = A("s_m2", [128, NT_ALL], F32)
    s_rrow = A("s_rrow", [128, 1024], BF16)
    s_batt = A("s_batt", [64, 1024], F32)
    s_pvout = [A(f"s_pvout{h}", [65, 512], F32) for h in range(2)]
    s_gm1 = A("s_gm1", [128, NT_OWN * 4], F32)
    s_gm2 = A("s_gm2", [128, NT_OWN * 4], F32)
    s_gs = A("s_gs", [128, NT_OWN * 4], F32)
    s_gred = A("s_gred", [128, NT_OWN], F32)
    s_rw = A("s_rw", [128, NCH * E], BF16)
    s_bofm = A("s_bofm", [128, NCH], F32)
    s_topm = A("s_topm", [128, NT_OWN], F32)
    s_den = A("s_den", [128, NT_OWN], F32)

    used = nc.SBUF_PARTITION_SIZE_BYTES - nc.sbuf_bytes_remaining
    base = (used + 95) // 32 * 32
    cap = 229344

    def at(name, shape, dtype, off):
        nbytes = {F32: 4, BF16: 2}[dtype] * shape[-1]
        assert base + off + nbytes <= cap, (name, base, off, nbytes)
        return nc.alloc_sbuf_tensor_at(name, shape, dtype, offset=base + off)

    xn_other = [at(f"xn_other{c}", [128, TOWN], BF16, 2 * TOWN * c) for c in range(NCH)]
    o1 = 2 * TOWN * NCH
    s_wq = [at(f"s_wq{i}", [128, DM], BF16, o1 + 2048 * i) for i in range(2)]
    s_wk = [at(f"s_wk{i}", [128, DM], BF16, o1 + 4096 + 2048 * i) for i in range(2)]
    s_wv = [at(f"s_wv{i}", [128, DM], BF16, o1 + 8192 + 2048 * i) for i in range(2)]
    o2 = o1 + 12288
    s_kp = at("s_kp", [128, SEQ], BF16, o2)
    o3 = o2 + 2 * SEQ
    s_qp = at("s_qp", [128, TOWN], BF16, o3)
    o4 = o3 + 2 * TOWN
    s_vaug = [at(f"s_vaug{h}", [128, NT_ALL * 65], BF16, o4 + h * (NT_ALL * 65 * 2 + 32))
              for h in range(2)]
    o5 = o4 + 2 * (NT_ALL * 65 * 2 + 32)
    s_vtmp = [at(f"s_vtmp{i}", [128, 512], BF16, o5 + 1024 * i) for i in range(2)]
    o6 = o5 + 2048
    s_apair = at("s_apair", [128, TOWN], BF16, o6)
    s_junk = at("s_junk", [128, DM], F32, o6)
    out_ring = [at("out_ring0", [128, DM], F32, o6)]
    o7 = o6 + max(2 * TOWN, 4096)
    s_apairB = at("s_apairB", [64, TOWN], BF16, o7)
    x_ring.append(at("x_ring2", [128, DM], F32, o7))
    o8 = o7 + max(2 * TOWN, 4096)
    out_ring.append(at("out_ring1", [128, DM], F32, o8))
    out_ring.append(at("out_ring2", [128, DM], F32, 49152))
    out_ring.append(at("out_ring3", [128, DM], F32, 49152 + 4096))
    s_wo = at("s_wo", [128, NPAIR * DM], BF16, o8)
    o9 = o8 + 2 * NPAIR * DM
    assert base + o9 <= cap, (base, o9)

    s_gate = [at(f"s_gate{i}", [128, NCH * IMOE], BF16, 8192 * i) for i in range(2)]
    s_up = [at(f"s_up{i}", [128, NCH * IMOE], BF16, 16384 + 8192 * i) for i in range(2)]
    s_down = [at(f"s_down{i}", [128, NI * DM], BF16, 32768 + 8192 * i) for i in range(2)]
    s_be = [at(f"s_be{i}", [128, TOWN], BF16, 49152 + 2 * TOWN * i) for i in range(2)]
    m1 = 49152 + 4 * TOWN
    s_hv = [at(f"s_hv{i}", [128, TOWN], BF16, m1 + 2 * TOWN * i) for i in range(NI)]
    m2 = m1 + 2 * TOWN * NI
    s_gsb = [at(f"s_gsb{i}", [128, 512], BF16, m2 + 1024 * i) for i in range(2)]
    s_usb = [at(f"s_usb{i}", [128, 512], BF16, m2 + 2048 + 1024 * i) for i in range(2)]
    m3 = m2 + 4096
    sz = NT_OWN * E * 4
    s_scores = at("s_scores", [128, NT_OWN * E], F32, m3)
    s_sfc = at("s_sfc", [128, NT_OWN * E], F32, m3 + sz)
    s_tmp = at("s_tmp", [128, NT_OWN * E], F32, m3 + 2 * sz)
    s_selm = at("s_selm", [128, NT_OWN * E], F32, m3 + 3 * sz)
    s_ism = at("s_ism", [128, NT_OWN * E], F32, m3 + 4 * sz)
    s_wtm = at("s_wtm", [128, NT_OWN * E], F32, m3 + 5 * sz)
    m4 = m3 + 6 * sz
    s_sel = at("s_sel", [E, E * 128], BF16, m4)
    m5 = m4 + E * 128 * 2
    assert base + m5 <= cap, (base, m5)

    s_sg1 = at("s_sg1", [128, NCH * 512], BF16, 0)
    s_sg2 = at("s_sg2", [128, NCH * 512], BF16, 8192)
    s_su1 = at("s_su1", [128, NCH * 512], BF16, 16384)
    s_su2 = at("s_su2", [128, NCH * 512], BF16, 24576)
    s_sd1 = at("s_sd1", [128, NSH * 512], BF16, 32768)
    s_sd2 = at("s_sd2", [128, NSH * 512], BF16, 40960)
    s_hvsh = [at(f"s_hvsh{i}", [128, TH], BF16, 49152 + 2 * TH * i) for i in range(NSH)]
    sh1 = 49152 + 2 * TH * NSH
    s_gssh = [at(f"s_gssh{i}", [128, 512], BF16, sh1 + 1024 * i) for i in range(2)]
    assert base + sh1 + 2048 <= cap

    PA = nc.alloc_psum_tensor("PA", [128, 2048], F32)
    PB = nc.alloc_psum_tensor("PB", [128, 1024], F32)
    PT = nc.alloc_psum_tensor("PT", [128, 2048], BF16, side="right")

    p = Plan(nc)
    T = {}

    for src, dst in ((identb_d, s_identb), (identf_d, s_identf), (tri_d, s_tri),
                     (bando_d, s_bando), (tilt_d, s_tilt), (ones_d, s_ones)):
        p.op("sync", lambda s, a=src, b=dst: s.dma_start(out=b[:], in_=a), sem="dc", dma=True)
    p.op("sync", lambda s: s.dma_start(out=s_junk[0:1, 0:DM], in_=bo_d), sem="dc", dma=True)
    p.op("sync", lambda s: s.dma_start(out=s_bofm[:], in_=bofm_d), sem="dc", dma=True)
    t_consts = p.op("sync", lambda s: s.dma_start(out=s_junk[32:33, 0:E], in_=rb_d),
                  sem="dc", dma=True)
    p.wait("vector", t_consts)
    p.wait("scalar", t_consts)
    t_wo = p.op("gpsimd", lambda g: g.dma_start(
        out=s_wo[:].rearrange("p (r n) -> p r n", r=NPAIR),
        in_=wo_d.rearrange("(r p) n -> p r n", p=128)), sem="dwo", dma=True)
    t_rw = p.op("gpsimd", lambda g: g.dma_start(
        out=s_rw[:].rearrange("p (c e) -> p c e", c=NCH),
        in_=rw_d.rearrange("(c p) e -> p c e", p=128)), sem="drw", dma=True)
    for dram0, buf0 in ((wq_d, s_wq[0]), (wk_d, s_wk[0]), (wv_d, s_wv[0])):
        p.op("gpsimd", lambda g, dram0=dram0, buf0=buf0: g.dma_start(
            out=buf0[:].rearrange("p (c n) -> p c n", c=NCH),
            in_=dram0.rearrange("(c p) n -> p c n", p=128)[:, :, 0:128]),
            sem="dwa0", dma=True)
    T["w0"] = ("dwa0", p.counts["dwa0"])
    t_onescol = p.op("gpsimd", lambda g: g.memset(s_onescol[:], 1.0), sem="pool")
    t_onesb = p.op("gpsimd", lambda g: g.memset(s_onesb[:], 1.0), sem="pool")
    pz_pbuf = [p.op("gpsimd", lambda g, b=b: g.memset(b[:], 0.0), sem="pool")
               for b in pbuf]
    pz_vaug = [p.op("gpsimd", lambda g, h=h: g.memset(s_vaug[h][:], 1.0), sem="pool")
               for h in range(2)]

    p.wait("tensor", t_consts)
    p.op("tensor", lambda t: t.matmul(PA[:, 0:512], s_ones[0:1, :], s_junk[0:1, 0:512],
                                      start=True, stop=True))
    p.op("tensor", lambda t: t.matmul(PA[:, 512:1024], s_ones[0:1, :],
                                      s_junk[0:1, 512:1024], start=True, stop=True))
    t_bcmm = p.op("tensor", lambda t: t.matmul(PA[:, DM:DM + E], s_ones[32:33, :],
                                               s_junk[32:33, 0:E],
                                               start=True, stop=True,
                                               tile_position=(32, 0)), sem="pe")
    p.wait("scalar", t_bcmm)
    p.op("scalar", lambda s: s.copy(out=s_bo_bc[:], in_=PA[:, 0:DM]))
    t_bc = p.op("scalar", lambda s: s.copy(out=s_rb_bc[:], in_=PA[:, DM:DM + E]),
                sem="act")

    slot_free = [None, t_bcmm, None]
    t_sq_prev = t_bcmm
    xtm_free = [None, None]
    pt_free = [None, None]
    ln_evac = [None] * NT_ALL
    ln_xring_last = [None, None, None]
    kv0_bank_free = [t_bc, t_bc]
    T["kq0"] = {}
    st_a = {}
    for tt_ in range(NT_ALL + 2):
        if tt_ < NT_ALL:
            t = tt_
            sl = t % 3
            p.wait("sync", slot_free[sl])
            tok_ld = p.op("sync", lambda s, t=t, sl=sl: s.dma_start(
                out=x_ring[sl][:], in_=x_seq[128 * t:128 * (t + 1), :]),
                sem=f"dx{sl}", dma=True)
            p.wait("scalar", tok_ld)
            p.wait("scalar", t_sq_prev)
            ts0 = p.op("scalar", lambda s, t=t, sl=sl: s.activation(
                out=s_junk[:], in_=x_ring[sl][:], func=AF.Square,
                accum_out=s_sumsq[:, t:t + 1]), sem="act")
            p.wait("scalar", ts0)
            ts_ = p.op("scalar", lambda s, t=t, sl=sl: s.activation(
                out=s_junk[:], in_=x_ring[sl][:], func=AF.Copy,
                accum_out=s_sum[:, t:t + 1]), sem="act")
            t_sq_prev = ts_
            st_a[t] = [tok_ld, ts_, ts_]
        if 1 <= tt_ and tt_ - 1 < NT_ALL:
            t = tt_ - 1
            p.wait("scalar", st_a[t][1])
            c1 = p.op("scalar", lambda s, t=t: s.mul(
                out=s_mean[:, t:t + 1], in_=s_sum[:, t:t + 1], mul=1.0 / DM), sem="act")
            p.wait("scalar", c1)
            c2 = p.op("scalar", lambda s, t=t: s.activation(
                out=s_m2[:, t:t + 1], in_=s_mean[:, t:t + 1], func=AF.Square), sem="act")
            p.wait("scalar", c2)
            c3 = p.op("scalar", lambda s, t=t: s.activation(
                out=s_m2[:, t:t + 1], in_=s_m2[:, t:t + 1], func=AF.Copy,
                bias=1e-5, scale=-1.0), sem="act")
            p.wait("scalar", c3)
            c4 = p.op("scalar", lambda s, t=t: s.activation(
                out=s_rstd[:, t:t + 1], in_=s_sumsq[:, t:t + 1], func=AF.Relu,
                bias=s_m2[:, t:t + 1], scale=1.0 / DM), sem="act")
            p.wait("scalar", c4)
            s1 = p.op("scalar", lambda s, t=t: s.activation(
                out=s_rstd[:, t:t + 1], in_=s_rstd[:, t:t + 1], func=AF.Sqrt), sem="act")
            st_a[t].append(s1)
        if tt_ >= 2:
            t = tt_ - 2
            sl = t % 3
            xr = t % 2
            s1 = st_a[t][3]
            p.wait("vector", s1)
            c5 = p.op("vector", lambda v, t=t: v.reciprocal(
                s_rstd[:, t:t + 1], s_rstd[:, t:t + 1]), sem="dve")
            p.wait("vector", c5)
            p.wait("vector", xtm_free[xr])
            tn = p.op("vector", lambda v, t=t, sl=sl, xr=xr: v.tensor_scalar(
                out=xn_tm[xr][:], in0=x_ring[sl][:], scalar1=s_mean[:, t:t + 1],
                scalar2=s_rstd[:, t:t + 1], op0=ALU.subtract, op1=ALU.mult), sem="dve")
            slot_free[sl] = tn
            ln_xring_last[sl] = tn
            p.wait("tensor", tn)
            p.wait("tensor", pt_free[xr])
            ttr = None
            for c in range(NCH):
                ttr = p.op("tensor", lambda tr, xr=xr, c=c: tr.transpose(
                    PT[:, 1024 * xr + 128 * c:1024 * xr + 128 * (c + 1)],
                    xn_tm[xr][:, 128 * c:128 * (c + 1)], s_identb[:]),
                    sem="pe" if c == NCH - 1 else None)
            xtm_free[xr] = ttr
            dst = xn_own if t < NT_OWN else xn_other
            tloc = t if t < NT_OWN else t - NT_OWN
            p.wait("scalar", ttr)
            ev = None
            for c in range(NCH):
                ev = p.op("scalar", lambda s, xr=xr, c=c, dst=dst, tloc=tloc: s.copy(
                    out=dst[c][:, 128 * tloc:128 * (tloc + 1)],
                    in_=PT[:, 1024 * xr + 128 * c:1024 * xr + 128 * (c + 1)]),
                    sem="act" if c == NCH - 1 else None)
            pt_free[xr] = ev
            ln_evac[t] = ev
            if t % 4 == 3:
                tsl0 = t // 4
                for kind in ("k", "q"):
                    if kind == "q" and tsl0 >= NT_OWN // 4:
                        continue
                    wbuf = s_wk[0] if kind == "k" else s_wq[0]
                    dstb = s_kp if kind == "k" else s_qp
                    bk = (2 * tsl0 + (0 if kind == "k" else 1)) % 2
                    p.wait("tensor", T["w0"])
                    p.wait("tensor", kv0_bank_free[bk])
                    p.wait("tensor", ev)
                    mmk = None
                    for c in range(NCH):
                        def _xc(c2, col0):
                            if col0 < TOWN:
                                return xn_own[c2][:, col0:col0 + 512]
                            return xn_other[c2][:, col0 - TOWN:col0 - TOWN + 512]
                        mmk = p.op("tensor", lambda tt2, c=c, tsl0=tsl0, bk=bk, wbuf=wbuf: tt2.matmul(
                            PA[:, 512 * bk:512 * (bk + 1)],
                            wbuf[:, 128 * c:128 * (c + 1)],
                            (xn_own[c][:, 512 * tsl0:512 * tsl0 + 512]
                             if 512 * tsl0 < TOWN else
                             xn_other[c][:, 512 * tsl0 - TOWN:512 * tsl0 - TOWN + 512]),
                            start=(c == 0), stop=(c == NCH - 1)),
                            sem="pe" if c == NCH - 1 else None)
                    p.wait("vector", mmk)
                    evk = p.op("vector", lambda v, tsl0=tsl0, bk=bk, dstb=dstb: v.tensor_copy(
                        out=dstb[:, 512 * tsl0:512 * (tsl0 + 1)],
                        in_=PA[:, 512 * bk:512 * (bk + 1)]), sem="dve")
                    kv0_bank_free[bk] = evk
                    T["kq0"][kind + str(tsl0)] = evk
    T["ln_done"] = ln_evac[NT_ALL - 1]

    def xn_col(chunk, col0, n):
        if col0 + n <= TOWN:
            return xn_own[chunk][:, col0:col0 + n]
        assert col0 >= TOWN
        return xn_other[chunk][:, col0 - TOWN:col0 - TOWN + n]

    wbuf_free = [None, None]
    kvq_free = None
    sband_free = [None, None]
    pvgate = [None, None]
    pbuf_free = {i: pz_pbuf[i] for i in range(6)}
    vaug_first = {0: pz_vaug[0], 1: pz_vaug[1]}
    pt_kv_free = [None, None]
    vtmp_free = [None, None]
    oacc_prev = None
    pvout_free = [None, None]
    rrow_free = [None, None]
    batt_free = [None, None]
    wo_prev_pe = None
    apB_prev = [None, None, None, None]
    wo_state = {}

    def emit_wo(pair_w, t_apB):
        nonlocal oacc_prev, sband_free
        p.wait("tensor", wo_state["t_wo_dma"])
        wo_state["t_wo_dma"] = None
        wo_bank_free = wo_state.get("wo_bank_free", [None, None])
        ta = None
        tw = None
        for oc in range(NCH):
            for sl2 in range(NSL):
                bank = (oc * NSL + sl2) % 2
                p.wait("tensor", wo_bank_free[bank])
                p.wait("tensor", wo_state["last_tat"])
                p.wait("tensor", t_apB)
                p.wait("tensor", sband_free[0])
                tw = p.op("tensor", lambda t, pair=pair_w, oc=oc, sl2=sl2, bank=bank: t.matmul(
                    PA[:, 512 * bank:512 * (bank + 1)],
                    s_wo[:, DM * pair + 128 * oc:DM * pair + 128 * (oc + 1)],
                    s_apair[:, 512 * sl2:512 * (sl2 + 1)],
                    start=True, stop=True), sem="pe")
                p.wait("vector", tw)
                p.wait("vector", oacc_prev)
                if pair_w == 0:
                    ta = p.op("vector", lambda v, oc=oc, sl2=sl2, bank=bank: v.tensor_copy(
                        out=o_acc[oc][:, 512 * sl2:512 * (sl2 + 1)],
                        in_=PA[:, 512 * bank:512 * (bank + 1)]), sem="dve")
                else:
                    ta = p.op("vector", lambda v, oc=oc, sl2=sl2, bank=bank: v.scalar_tensor_tensor(
                        out=o_acc[oc][:, 512 * sl2:512 * (sl2 + 1)],
                        in0=PA[:, 512 * bank:512 * (bank + 1)], scalar=1.0,
                        in1=o_acc[oc][:, 512 * sl2:512 * (sl2 + 1)],
                        op0=ALU.mult, op1=ALU.add), sem="dve")
                wo_bank_free[bank] = ta
        oacc_prev = ta
        wo_state["wo_bank_free"] = wo_bank_free
        sband_free[0] = ta
        return tw

    wo_state["t_wo_dma"] = t_wo
    for pair in range(NPAIR):
        ws = pair % 2
        if pair > 0:
            for dram, buf in ((wq_d, s_wq[ws]), (wk_d, s_wk[ws]), (wv_d, s_wv[ws])):
                p.wait("gpsimd", wbuf_free[ws])
                p.op("gpsimd", lambda g, dram=dram, buf=buf, pair=pair: g.dma_start(
                    out=buf[:].rearrange("p (c n) -> p c n", c=NCH),
                    in_=dram.rearrange("(c p) n -> p c n", p=128)[:, :, 128 * pair:128 * (pair + 1)]),
                    sem=f"dwa{ws}", dma=True)
            t_w = (f"dwa{ws}", p.counts[f"dwa{ws}"])
        else:
            t_w = T["w0"]

        p.wait("tensor", t_w)
        p.wait("tensor", kvq_free)
        p.wait("tensor", sband_free[0])
        kv_bank_free = list(kv0_bank_free) if pair == 0 else [None, None]
        for tsl in range(0 if pair > 0 else SEQ, SEQ // 512):
            bank = tsl % 2
            p.wait("tensor", ln_evac[4 * tsl + 3])
            p.wait("tensor", kv_bank_free[bank])
            mm = None
            for c in range(NCH):
                mm = p.op("tensor", lambda t, c=c, tsl=tsl, bank=bank, ws=ws: t.matmul(
                    PA[:, 512 * bank:512 * (bank + 1)],
                    s_wk[ws][:, 128 * c:128 * (c + 1)], xn_col(c, 512 * tsl, 512),
                    start=(c == 0), stop=(c == NCH - 1)),
                    sem="pe" if c == NCH - 1 else None)
            p.wait("vector", mm)
            ev = p.op("vector", lambda v, tsl=tsl, bank=bank: v.tensor_copy(
                out=s_kp[:, 512 * tsl:512 * (tsl + 1)],
                in_=PA[:, 512 * bank:512 * (bank + 1)]), sem="dve")
            kv_bank_free[bank] = ev
        vpend = None

        def flush_vtrans():
            nonlocal vpend
            if vpend is None:
                return
            vs0, tsl0, ev_v0 = vpend
            vpend = None
            p.wait("tensor", ev_v0)
            p.wait("tensor", T["ln_done"])
            p.wait("tensor", pt_kv_free[tsl0 % 2])
            ev2 = None
            for b in range(4):
                ptc = 1024 * (tsl0 % 2) + 128 * b
                tt = p.op("tensor", lambda t, vs0=vs0, b=b, ptc=ptc: t.transpose(
                    PT[:, ptc:ptc + 128], s_vtmp[vs0][:, 128 * b:128 * (b + 1)],
                    s_identb[:]), sem="pe")
                kt2 = 4 * tsl0 + b
                p.wait("vector", tt)
                for h in range(2):
                    p.wait("vector", vaug_first[h])
                    vaug_first[h] = None
                    ev2 = p.op("vector", lambda v, h=h, kt2=kt2, ptc=ptc: v.tensor_copy(
                        out=s_vaug[h][:, 65 * kt2:65 * kt2 + 64],
                        in_=PT[:, ptc + 64 * h:ptc + 64 * (h + 1)]), sem="dve")
                vtmp_free[vs0] = tt
            pt_kv_free[tsl0 % 2] = ev2

        for tsl in range(SEQ // 512):
            bank = tsl % 2
            vs = tsl % 2
            p.wait("tensor", ln_evac[4 * tsl + 3])
            p.wait("tensor", kv_bank_free[bank])
            mm = None
            for c in range(NCH):
                mm = p.op("tensor", lambda t, c=c, tsl=tsl, bank=bank, ws=ws: t.matmul(
                    PA[:, 512 * bank:512 * (bank + 1)],
                    s_wv[ws][:, 128 * c:128 * (c + 1)], xn_col(c, 512 * tsl, 512),
                    start=(c == 0), stop=(c == NCH - 1)),
                    sem="pe" if c == NCH - 1 else None)
            p.wait("vector", mm)
            p.wait("vector", vtmp_free[vs])
            ev_v = p.op("vector", lambda v, vs=vs, bank=bank: v.tensor_copy(
                out=s_vtmp[vs][:], in_=PA[:, 512 * bank:512 * (bank + 1)]), sem="dve")
            kv_bank_free[bank] = ev_v
            flush_vtrans()
            vpend = (vs, tsl, ev_v)
        flush_vtrans()
        for tsl in range(0 if pair > 0 else NSL, NSL):
            bank = tsl % 2
            p.wait("tensor", kv_bank_free[bank])
            mm = None
            for c in range(NCH):
                mm = p.op("tensor", lambda t, c=c, tsl=tsl, bank=bank, ws=ws: t.matmul(
                    PA[:, 512 * bank:512 * (bank + 1)],
                    s_wq[ws][:, 128 * c:128 * (c + 1)],
                    xn_own[c][:, 512 * tsl:512 * (tsl + 1)],
                    start=(c == 0), stop=(c == NCH - 1)),
                    sem="pe" if c == NCH - 1 else None)
            p.wait("vector", mm)
            ev = p.op("vector", lambda v, tsl=tsl, bank=bank: v.tensor_copy(
                out=s_qp[:, 512 * tsl:512 * (tsl + 1)],
                in_=PA[:, 512 * bank:512 * (bank + 1)]), sem="dve")
            kv_bank_free[bank] = ev
        wbuf_free[ws] = mm
        t_kvq = ("dve", p.counts["dve"])
        if pair == NPAIR - 1:
            T["kvq_last"] = ("pe", p.counts["pe"])
        sband_free[0] = kv_bank_free[1]

        if pair > 0:
            wo_prev_pe = emit_wo(pair - 1, apB_prev)

        last_tatB = None
        epi = None

        def flush_epi():
            nonlocal epi, last_tatB, rrow_free, batt_free, pvout_free, pvgate
            if epi is None:
                return
            trecA, trecB, ej = epi
            epi = None
            p.wait("tensor", trecA)
            p.wait("tensor", t_onesb)
            tbcA = p.op("tensor", lambda t: t.matmul(
                PB[0:64, 0:512], s_onesb[64:65, :], s_rrow[64:65, 0:512],
                start=True, stop=True, tile_position=(64, 0)), sem="pe")
            p.wait("tensor", trecB)
            tbcB = p.op("tensor", lambda t: t.matmul(
                PB[0:64, 512:1024], s_onesb[64:65, :], s_rrow[64:65, 512:1024],
                start=True, stop=True, tile_position=(64, 0)), sem="pe")
            rrow_free = [tbcA, tbcB]
            p.wait("scalar", tbcA)
            p.wait("scalar", batt_free[0])
            tbeA = p.op("scalar", lambda s: s.copy(out=s_batt[:, 0:512],
                                                   in_=PB[0:64, 0:512]), sem="act")
            p.wait("scalar", tbcB)
            p.wait("scalar", batt_free[1])
            tbeB = p.op("scalar", lambda s: s.copy(out=s_batt[:, 512:1024],
                                                   in_=PB[0:64, 512:1024]), sem="act")
            pvgate = [tbeA, tbeB]
            p.wait("vector", tbeA)
            if pair > 0 and ej == 0:
                p.wait("vector", wo_prev_pe)
            tatA = p.op("vector", lambda v, ej=ej: v.tensor_tensor(
                out=s_apair[0:64, 512 * ej:512 * (ej + 1)],
                in0=s_pvout[0][0:64, :], in1=s_batt[:, 0:512], op=ALU.mult), sem="dve")
            p.wait("vector", tbeB)
            p.wait("vector", apB_prev[ej])
            tatB = p.op("vector", lambda v, ej=ej: v.tensor_tensor(
                out=s_apairB[:, 512 * ej:512 * (ej + 1)],
                in0=s_pvout[1][0:64, :], in1=s_batt[:, 512:1024], op=ALU.mult), sem="dve")
            pvout_free = [tatA, tatB]
            batt_free = [tatA, tatB]
            last_tatB = tatB
            p.wait("sync", tatB)
            apB_prev[ej] = p.op("sync", lambda s, ej=ej: s.dma_start(
                out=s_apair[64:128, 512 * ej:512 * (ej + 1)],
                in_=s_apairB[:, 512 * ej:512 * (ej + 1)]), sem=f"dap{ej}", dma=True)

        for js in range(NQS):
            kts = [(0, kt) for kt in range(4 * js + 4)] + \
                  [(1, kt) for kt in range(4 * js + 4)]
            n_kt = len(kts)
            tpvA = tpvB = None
            pending = None
            for idx in range(n_kt + 1):
                if idx == 1:
                    flush_epi()
                if idx < n_kt:
                    reg, kt = kts[idx]
                    r = kt - 4 * js
                    q0 = 128 * max(r, 0)
                    bp = idx % 2
                    kcol = 128 * kt + (TOWN if reg == 1 else 0)
                    p.wait("tensor", sband_free[bp])
                    p.wait("tensor", t_kvq)
                    p.op("tensor", lambda t, kcol=kcol, js=js, q0=q0, bp=bp: t.matmul(
                        PA[:, 1024 * bp + q0:1024 * bp + 512],
                        s_kp[0:64, kcol:kcol + 128],
                        s_qp[0:64, 512 * js + q0:512 * (js + 1)],
                        start=True, stop=True, tile_position=(0, 0)))
                    tsb = p.op("tensor", lambda t, kcol=kcol, js=js, q0=q0, bp=bp: t.matmul(
                        PA[:, 1024 * bp + 512 + q0:1024 * (bp + 1)],
                        s_kp[64:128, kcol:kcol + 128],
                        s_qp[64:128, 512 * js + q0:512 * (js + 1)],
                        start=True, stop=True, tile_position=(64, 0)), sem="pe")
                    vb = idx % 2 if r < 0 else 2 + r
                    p.wait("scalar", tsb)
                    p.wait("scalar", pbuf_free[vb])
                    if r < 0:
                        texpA = p.op("scalar", lambda s, vb=vb, bp=bp: s.activation(
                            out=pbuf[vb][:, 0:512], in_=PA[:, 1024 * bp:1024 * bp + 512],
                            func=AF.Exp, scale=SCALE), sem="act")
                        texpB = p.op("scalar", lambda s, vb=vb, bp=bp: s.activation(
                            out=pbuf[vb][:, 512:1024],
                            in_=PA[:, 1024 * bp + 512:1024 * (bp + 1)],
                            func=AF.Exp, scale=SCALE), sem="act")
                    else:
                        texpA = p.op("scalar", lambda s, vb=vb, q0=q0, bp=bp: s.activation(
                            out=pbuf[vb][:, q0:512], in_=PA[:, 1024 * bp + q0:1024 * bp + 512],
                            func=AF.Exp, scale=SCALE), sem="act")
                        texpB = p.op("scalar", lambda s, vb=vb, q0=q0, bp=bp: s.activation(
                            out=pbuf[vb][:, 512 + q0:1024],
                            in_=PA[:, 1024 * bp + 512 + q0:1024 * (bp + 1)],
                            func=AF.Exp, scale=SCALE), sem="act")
                    sband_free[bp] = texpB
                    depA, depB = texpA, texpB
                    if r >= 0:
                        band = s_tri if reg == 0 else s_bando
                        p.wait("vector", texpA)
                        depA = p.op("vector", lambda v, vb=vb, q0=q0, band=band: v.tensor_tensor(
                            out=pbuf[vb][:, q0:q0 + 128], in0=pbuf[vb][:, q0:q0 + 128],
                            in1=band[:], op=ALU.mult), sem="dve")
                        p.wait("vector", texpB)
                        depB = p.op("vector", lambda v, vb=vb, q0=q0, band=band: v.tensor_tensor(
                            out=pbuf[vb][:, 512 + q0:512 + q0 + 128],
                            in0=pbuf[vb][:, 512 + q0:512 + q0 + 128],
                            in1=band[:], op=ALU.mult), sem="dve")
                    kt2 = kt + (NT_OWN if reg == 1 else 0)
                    cur = (kt2, vb, depA, depB, idx == 0, idx == n_kt - 1)
                else:
                    cur = None
                if pending is not None:
                    kt2, vb, depA, depB, first, last = pending
                    p.wait("tensor", depA)
                    if first:
                        p.wait("tensor", pvgate[0])
                    tpvA = p.op("tensor", lambda t, kt2=kt2, vb=vb, first=first, last=last: t.matmul(
                        PB[0:65, 0:512], s_vaug[0][:, 65 * kt2:65 * (kt2 + 1)],
                        pbuf[vb][:, 0:512], start=first, stop=last),
                        sem="pe" if last else None)
                    p.wait("tensor", depB)
                    if first:
                        p.wait("tensor", pvgate[1])
                    tpvB = p.op("tensor", lambda t, kt2=kt2, vb=vb, first=first, last=last: t.matmul(
                        PB[0:65, 512:1024], s_vaug[1][:, 65 * kt2:65 * (kt2 + 1)],
                        pbuf[vb][:, 512:1024], start=first, stop=last), sem="pe")
                    pbuf_free[vb] = tpvB
                pending = cur
            p.wait("scalar", tpvA)
            p.wait("scalar", tpvB)
            p.wait("scalar", pvout_free[0])
            tevA = p.op("scalar", lambda s: s.copy(out=s_pvout[0][:], in_=PB[0:65, 0:512]),
                        sem="act")
            p.wait("scalar", pvout_free[1])
            tevB = p.op("scalar", lambda s: s.copy(out=s_pvout[1][:], in_=PB[0:65, 512:1024]),
                        sem="act")
            p.wait("vector", tevA)
            p.wait("vector", rrow_free[0])
            def _recA(v):
                with nc.allow_low_precision(reason="bf16 softmax-recip broadcast"):
                    return v.reciprocal(s_rrow[64:65, 0:512], s_pvout[0][64:65, :])
            trecA = p.op("vector", _recA, sem="dve")
            p.wait("vector", tevB)
            p.wait("vector", rrow_free[1])
            def _recB(v):
                with nc.allow_low_precision(reason="bf16 softmax-recip broadcast"):
                    return v.reciprocal(s_rrow[64:65, 512:1024], s_pvout[1][64:65, :])
            trecB = p.op("vector", _recB, sem="dve")
            epi = (trecA, trecB, js)
        flush_epi()
        t_apB = list(apB_prev)
        kvq_free = ("pe", p.counts["pe"])
        wo_state["last_tat"] = last_tatB

    t_sig_last = None
    rt_bank_free = [pvgate[0], pvgate[1]]
    for t in range(NT_OWN):
        bank = t % 2
        p.wait("tensor", t_rw)
        p.wait("tensor", rt_bank_free[bank])
        mm = None
        for c in range(NCH):
            mm = p.op("tensor", lambda tt, t=t, c=c, bank=bank: tt.matmul(
                PB[0:128, 512 * bank:512 * bank + E],
                xn_own[c][:, 128 * t:128 * (t + 1)], s_rw[:, E * c:E * (c + 1)],
                start=(c == 0), stop=(c == NCH - 1)), sem="pe" if c == NCH - 1 else None)
        p.wait("scalar", mm)
        p.wait("scalar", t_sig_last)
        t_sig_last = p.op("scalar", lambda s, t=t, bank=bank: s.activation(
            out=s_scores[:, E * t:E * (t + 1)], in_=PB[0:128, 512 * bank:512 * bank + E],
            func=AF.Sigmoid), sem="act")
        rt_bank_free[bank] = t_sig_last
    T["rt_done"] = t_sig_last

    wo_state["last_tat"] = last_tatB
    emit_wo(NPAIR - 1, apB_prev)
    T["attn_done"] = oacc_prev
    t_attn_pe = ("pe", p.counts["pe"])


    p.wait("sync", t_attn_pe)
    t_sel = p.op("sync", lambda s: s.dma_start(
        out=s_sel[:].rearrange("k (e c) -> k e c", e=E), in_=sel_d), sem="dsel", dma=True)

    def stage(fns):
        tok = None
        for i, f in enumerate(fns):
            tok = p.op("vector", f, sem="dve" if i == len(fns) - 1 else None)
        return tok

    p.wait("vector", t_sig_last)
    tok = stage([(lambda v, t=t: v.tensor_tensor(
        out=s_sfc[:, E * t:E * (t + 1)], in0=s_scores[:, E * t:E * (t + 1)],
        in1=s_rb_bc[:], op=ALU.add)) for t in range(NT_OWN)])
    p.wait("vector", tok)
    tok = stage([(lambda v, t=t, g=g: v.reduce_max(
        s_gm1[:, 4 * t + g:4 * t + g + 1],
        s_sfc[:, E * t + 8 * g:E * t + 8 * (g + 1)], axis=AX.X))
        for t in range(NT_OWN) for g in range(4)])
    p.wait("vector", tok)
    tok = stage([(lambda v, t=t, g=g: v.tensor_scalar(
        out=s_ism[:, E * t + 8 * g:E * t + 8 * (g + 1)],
        in0=s_sfc[:, E * t + 8 * g:E * t + 8 * (g + 1)],
        scalar1=s_gm1[:, 4 * t + g:4 * t + g + 1], scalar2=None, op0=ALU.is_ge))
        for t in range(NT_OWN) for g in range(4)])
    p.wait("vector", tok)
    tok = stage([(lambda v, t=t, g=g: v.scalar_tensor_tensor(
        out=s_tmp[:, E * t + 8 * g:E * t + 8 * (g + 1)],
        in0=s_ism[:, E * t + 8 * g:E * t + 8 * (g + 1)], scalar=NEG,
        in1=s_sfc[:, E * t + 8 * g:E * t + 8 * (g + 1)], op0=ALU.mult, op1=ALU.add))
        for t in range(NT_OWN) for g in range(4)])
    p.wait("vector", tok)
    tok = stage([(lambda v, t=t, g=g: v.reduce_max(
        s_gm2[:, 4 * t + g:4 * t + g + 1],
        s_tmp[:, E * t + 8 * g:E * t + 8 * (g + 1)], axis=AX.X))
        for t in range(NT_OWN) for g in range(4)])
    p.wait("vector", tok)
    tok = stage([(lambda v, t=t: v.tensor_tensor(
        out=s_gs[:, 4 * t:4 * (t + 1)], in0=s_gm1[:, 4 * t:4 * (t + 1)],
        in1=s_gm2[:, 4 * t:4 * (t + 1)], op=ALU.add)) for t in range(NT_OWN)])
    p.wait("vector", tok)
    tok = stage([(lambda v, t=t: v.reduce_max(
        s_gred[:, t:t + 1], s_gs[:, 4 * t:4 * (t + 1)], axis=AX.X))
        for t in range(NT_OWN)])
    p.wait("vector", tok)
    tok = stage([(lambda v, t=t: v.tensor_scalar(
        out=s_gm1[:, 4 * t:4 * (t + 1)], in0=s_gs[:, 4 * t:4 * (t + 1)],
        scalar1=s_gred[:, t:t + 1], scalar2=None, op0=ALU.is_ge))
        for t in range(NT_OWN)])
    p.wait("vector", tok)
    tok = stage([(lambda v, t=t: v.scalar_tensor_tensor(
        out=s_gm2[:, 4 * t:4 * (t + 1)], in0=s_gm1[:, 4 * t:4 * (t + 1)], scalar=NEG,
        in1=s_gs[:, 4 * t:4 * (t + 1)], op0=ALU.mult, op1=ALU.add))
        for t in range(NT_OWN)])
    p.wait("vector", tok)
    tok = stage([(lambda v, t=t: v.reduce_max(
        s_gred[:, t:t + 1], s_gm2[:, 4 * t:4 * (t + 1)], axis=AX.X))
        for t in range(NT_OWN)])
    p.wait("vector", tok)
    tok = stage([(lambda v, t=t: v.tensor_scalar(
        out=s_gm1[:, 4 * t:4 * (t + 1)], in0=s_gs[:, 4 * t:4 * (t + 1)],
        scalar1=s_gred[:, t:t + 1], scalar2=None, op0=ALU.is_ge))
        for t in range(NT_OWN)])
    p.wait("vector", tok)
    tok = stage([(lambda v, t=t, g=g: v.tensor_scalar(
        out=s_tmp[:, E * t + 8 * g:E * t + 8 * (g + 1)],
        in0=s_sfc[:, E * t + 8 * g:E * t + 8 * (g + 1)],
        scalar1=s_gm1[:, 4 * t + g:4 * t + g + 1], scalar2=None, op0=ALU.mult))
        for t in range(NT_OWN) for g in range(4)])
    p.wait("vector", tok)
    tok = stage([(lambda v, t=t: v.tensor_tensor(
        out=s_tmp[:, E * t:E * (t + 1)], in0=s_tmp[:, E * t:E * (t + 1)],
        in1=s_tilt[:], op=ALU.add)) for t in range(NT_OWN)])
    for it in range(4):
        p.wait("vector", tok)
        tok = stage([(lambda v, t=t: v.reduce_max(
            s_topm[:, t:t + 1], s_tmp[:, E * t:E * (t + 1)], axis=AX.X))
            for t in range(NT_OWN)])
        p.wait("vector", tok)
        tok = stage([(lambda v, t=t: v.tensor_scalar(
            out=s_ism[:, E * t:E * (t + 1)], in0=s_tmp[:, E * t:E * (t + 1)],
            scalar1=s_topm[:, t:t + 1], scalar2=None, op0=ALU.is_ge))
            for t in range(NT_OWN)])
        p.wait("vector", tok)
        if it == 0:
            tok = stage([(lambda v, t=t: v.tensor_copy(
                out=s_selm[:, E * t:E * (t + 1)], in_=s_ism[:, E * t:E * (t + 1)]))
                for t in range(NT_OWN)])
        else:
            tok = stage([(lambda v, t=t: v.tensor_tensor(
                out=s_selm[:, E * t:E * (t + 1)], in0=s_selm[:, E * t:E * (t + 1)],
                in1=s_ism[:, E * t:E * (t + 1)], op=ALU.add)) for t in range(NT_OWN)])
        if it < 3:
            p.wait("vector", tok)
            tok = stage([(lambda v, t=t: v.scalar_tensor_tensor(
                out=s_tmp[:, E * t:E * (t + 1)], in0=s_ism[:, E * t:E * (t + 1)],
                scalar=NEG, in1=s_tmp[:, E * t:E * (t + 1)],
                op0=ALU.mult, op1=ALU.add)) for t in range(NT_OWN)])
    p.wait("vector", tok)
    tok = stage([(lambda v, t=t: v.tensor_tensor(
        out=s_wtm[:, E * t:E * (t + 1)], in0=s_scores[:, E * t:E * (t + 1)],
        in1=s_selm[:, E * t:E * (t + 1)], op=ALU.mult)) for t in range(NT_OWN)])
    p.wait("vector", tok)
    tok = stage([(lambda v, t=t: v.reduce_sum(
        s_den[:, t:t + 1], s_wtm[:, E * t:E * (t + 1)], axis=AX.X))
        for t in range(NT_OWN)])
    p.wait("vector", tok)
    tok = p.op("vector", lambda v: v.tensor_scalar(out=s_den[:], in0=s_den[:],
                                                   scalar1=1e-20, scalar2=None,
                                                   op0=ALU.add), sem="dve")
    p.wait("vector", tok)
    tok = p.op("vector", lambda v: v.reciprocal(s_den[:], s_den[:]), sem="dve")
    p.wait("vector", tok)
    t_w_ready = stage([(lambda v, t=t: v.tensor_scalar(
        out=s_wtm[:, E * t:E * (t + 1)], in0=s_wtm[:, E * t:E * (t + 1)],
        scalar1=s_den[:, t:t + 1], scalar2=2.5, op0=ALU.mult, op1=ALU.mult))
        for t in range(NT_OWN)])
    t_wfm = None
    wtr_free = None
    for t in range(NT_OWN):
        p.wait("tensor", t_w_ready)
        p.wait("tensor", wtr_free)
        p.wait("tensor", sband_free[1])
        ttr = p.op("tensor", lambda tt, t=t: tt.transpose(
            PA[0:E, 1024:1152], s_wtm[:, E * t:E * (t + 1)], s_identf[:]), sem="pe")
        p.wait("scalar", ttr)
        t_wfm = p.op("scalar", lambda s, t=t: s.copy(
            out=s_wfm[:, 128 * t:128 * (t + 1)], in_=PA[0:E, 1024:1152]), sem="act")
        wtr_free = t_wfm
    T["router_done"] = t_wfm
    t_router_pe = ("pe", p.counts["pe"])

    moe_w_free = [None, None]
    t_gate_dma = [None, None]
    t_up_dma = [None, None]
    t_down_dma = [None, None]
    be_bank_free = [T["rt_done"], T["rt_done"]]
    gs_free = [None, None]
    us_free = [None, None]
    hv_free = [None] * NI
    oacc_prev = T["attn_done"]
    dn_bank_free = [None, None]
    gu_bank_free = [sband_free[0], sband_free[0], t_wfm, sband_free[1]]

    def emit_be(e_b):
        bslot = e_b % 2
        t_be = None
        for sl in range(NSL):
            bank = sl % 2
            p.wait("tensor", T["router_done"])
            p.wait("tensor", t_sel)
            p.wait("tensor", be_bank_free[bank])
            p.wait("tensor", dn_bank_free[bank])
            tm = p.op("tensor", lambda t, e_b=e_b, sl=sl, bank=bank: t.matmul(
                PB[:, 512 * bank:512 * (bank + 1)], s_sel[:, 128 * e_b:128 * (e_b + 1)],
                s_wfm[:, 512 * sl:512 * (sl + 1)], start=True, stop=True), sem="pe")
            p.wait("scalar", tm)
            p.wait("scalar", be_slot_free[bslot] if sl == 0 else None)
            t_be = p.op("scalar", lambda s, sl=sl, bank=bank, bslot=bslot: s.copy(
                out=s_be[bslot][:, 512 * sl:512 * (sl + 1)],
                in_=PB[:, 512 * bank:512 * (bank + 1)]), sem="act")
            be_bank_free[bank] = t_be
        return t_be

    t_be_slot = [None, None]
    be_slot_free = [None, None]
    for e in range(E):
        ws = e % 2
        p.wait("gpsimd", moe_w_free[ws])
        if e < 2:
            p.wait("gpsimd", T["kvq_last"] if TOWN >= 2048 else t_attn_pe)
        p.op("gpsimd", lambda g, e=e, ws=ws: g.dma_start(
            out=s_gate[ws][:].rearrange("p (c i) -> p c i", c=NCH),
            in_=gate_d[e].rearrange("(c p) i -> p c i", p=128)), sem=f"dm{ws}", dma=True)
        p.op("gpsimd", lambda g, e=e, ws=ws: g.dma_start(
            out=s_up[ws][:].rearrange("p (c i) -> p c i", c=NCH),
            in_=up_d[e].rearrange("(c p) i -> p c i", p=128)), sem=f"dm{ws}", dma=True)
        if e == 1 or (e == 0 and TOWN < 2048):
            p.wait("gpsimd", t_attn_pe)
        tw3 = p.op("gpsimd", lambda g, e=e, ws=ws: g.dma_start(
            out=s_down[ws][:].rearrange("p (c h) -> p c h", c=NI),
            in_=down_d[e].rearrange("(c p) h -> p c h", p=128)), sem=f"dm{ws}", dma=True)
        t_gate_dma[ws] = t_up_dma[ws] = t_down_dma[ws] = tw3

        if e == 0:
            t_be_slot[0] = emit_be(0)
        t_be = t_be_slot[e % 2]
        ta = None
        for i in range(NI):
            for sl2 in range(NSL):
                col0 = 512 * sl2
                gb = sl2 % 2
                ub = 2 + sl2 % 2
                ring = (i * NSL + sl2) % 2
                p.wait("tensor", t_gate_dma[ws])
                p.wait("tensor", gu_bank_free[gb])
                mm = None
                for c in range(NCH):
                    mm = p.op("tensor", lambda t, ws=ws, c=c, i=i, col0=col0, gb=gb: t.matmul(
                        PA[:, 512 * gb:512 * (gb + 1)],
                        s_gate[ws][:, IMOE * c + 128 * i:IMOE * c + 128 * (i + 1)],
                        xn_own[c][:, col0:col0 + 512],
                        start=(c == 0), stop=(c == NCH - 1)),
                        sem="pe" if c == NCH - 1 else None)
                p.wait("scalar", mm)
                p.wait("scalar", gs_free[ring])
                tsil = p.op("scalar", lambda s, gb=gb, ring=ring: s.activation(
                    out=s_gsb[ring][:], in_=PA[:, 512 * gb:512 * (gb + 1)],
                    func=AF.Silu), sem="act")
                gu_bank_free[gb] = tsil
                p.wait("tensor", t_up_dma[ws])
                p.wait("tensor", gu_bank_free[ub])
                mm = None
                for c in range(NCH):
                    mm = p.op("tensor", lambda t, ws=ws, c=c, i=i, col0=col0, ub=ub: t.matmul(
                        PA[:, 512 * ub:512 * (ub + 1)],
                        s_up[ws][:, IMOE * c + 128 * i:IMOE * c + 128 * (i + 1)],
                        xn_own[c][:, col0:col0 + 512],
                        start=(c == 0), stop=(c == NCH - 1)),
                        sem="pe" if c == NCH - 1 else None)
                p.wait("vector", mm)
                p.wait("vector", t_be)
                p.wait("vector", us_free[ring])
                tus = p.op("vector", lambda v, ub=ub, ring=ring, col0=col0, e=e: v.scalar_tensor_tensor(
                    out=s_usb[ring][:], in0=PA[:, 512 * ub:512 * (ub + 1)], scalar=1.0,
                    in1=s_be[e % 2][:, col0:col0 + 512], op0=ALU.mult, op1=ALU.mult),
                    sem="dve")
                gu_bank_free[ub] = tus
                p.wait("vector", tus)
                p.wait("vector", tsil)
                p.wait("vector", hv_free[i])
                thv = p.op("vector", lambda v, i=i, ring=ring, col0=col0: v.tensor_tensor(
                    out=s_hv[i][:, col0:col0 + 512], in0=s_gsb[ring][:],
                    in1=s_usb[ring][:], op=ALU.mult), sem="dve")
                gs_free[ring] = thv
                us_free[ring] = thv
                hv_free[i] = thv
        if e == E - 2:
            T["gu30"] = ("pe", p.counts["pe"])
        if e == E - 1:
            T["gu31"] = ("pe", p.counts["pe"])
        be_slot_free[e % 2] = ("dve", p.counts["dve"])
        if e + 1 < E:
            t_be_slot[(e + 1) % 2] = emit_be(e + 1)
        t_benext = t_be_slot[(e + 1) % 2] if e + 1 < E else None
        t_hv_all = list(hv_free)
        for oc in range(NCH):
            for sl2 in range(NSL):
                col0 = 512 * sl2
                bank = (oc * NSL + sl2) % 2
                p.wait("tensor", t_down_dma[ws])
                p.wait("tensor", t_benext)
                p.wait("tensor", dn_bank_free[bank])
                p.wait("tensor", t_hv_all)
                mm = None
                for i in range(NI):
                    mm = p.op("tensor", lambda t, ws=ws, i=i, oc=oc, col0=col0, bank=bank: t.matmul(
                        PB[:, 512 * bank:512 * (bank + 1)],
                        s_down[ws][:, DM * i + 128 * oc:DM * i + 128 * (oc + 1)],
                        s_hv[i][:, col0:col0 + 512],
                        start=(i == 0), stop=(i == NI - 1)),
                        sem="pe" if i == NI - 1 else None)
                p.wait("vector", mm)
                p.wait("vector", oacc_prev)
                ta = p.op("vector", lambda v, oc=oc, col0=col0, bank=bank: v.scalar_tensor_tensor(
                    out=o_acc[oc][:, col0:col0 + 512],
                    in0=PB[:, 512 * bank:512 * (bank + 1)], scalar=1.0,
                    in1=o_acc[oc][:, col0:col0 + 512], op0=ALU.mult, op1=ALU.add),
                    sem="dve")
                dn_bank_free[bank] = ta
        hv_done = ("pe", p.counts["pe"])
        for i in range(NI):
            hv_free[i] = hv_done
        oacc_prev = ta
        moe_w_free[ws] = ("pe", p.counts["pe"])
        if e == E - 2:
            T["e30_done"] = moe_w_free[ws]
        if e == E - 1:
            T["e31_done"] = moe_w_free[ws]
    T["moe_done"] = oacc_prev
    t_moe_pe = ("pe", p.counts["pe"])

    big = TOWN >= 2048
    p.wait("gpsimd", T["gu30"] if big else t_moe_pe)
    t_sg1 = p.op("gpsimd", lambda g: g.dma_start(
        out=s_sg1[:].rearrange("p (c i) -> p c i", c=NCH),
        in_=sg_d.rearrange("(c p) i -> p c i", p=128)[:, :, 0:512]), sem="dsh", dma=True)
    p.wait("gpsimd", T["gu31"] if big else None)
    t_sg2 = p.op("gpsimd", lambda g: g.dma_start(
        out=s_sg2[:].rearrange("p (c i) -> p c i", c=NCH),
        in_=sg_d.rearrange("(c p) i -> p c i", p=128)[:, :, 512:1024]), sem="dsh", dma=True)
    t_su1 = p.op("gpsimd", lambda g: g.dma_start(
        out=s_su1[:].rearrange("p (c i) -> p c i", c=NCH),
        in_=su_d.rearrange("(c p) i -> p c i", p=128)[:, :, 0:512]), sem="dsh", dma=True)
    t_su2 = p.op("gpsimd", lambda g: g.dma_start(
        out=s_su2[:].rearrange("p (c i) -> p c i", c=NCH),
        in_=su_d.rearrange("(c p) i -> p c i", p=128)[:, :, 512:1024]), sem="dsh", dma=True)
    p.wait("gpsimd", T["e30_done"] if big else None)
    t_sd1 = p.op("gpsimd", lambda g: g.dma_start(
        out=s_sd1[:].rearrange("p (c h) -> p c h", c=NSH),
        in_=sd_d.rearrange("(c p) h -> p c h", p=128)[:, :, 0:512]), sem="dsh", dma=True)
    p.wait("gpsimd", T["e31_done"] if big else None)
    t_sd2 = p.op("gpsimd", lambda g: g.dma_start(
        out=s_sd2[:].rearrange("p (c h) -> p c h", c=NSH),
        in_=sd_d.rearrange("(c p) h -> p c h", p=128)[:, :, 512:1024]), sem="dsh", dma=True)

    t_shw = ("dsh", p.counts["dsh"])
    t_sg1 = t_sg2 = t_su1 = t_su2 = t_sd1 = t_sd2 = t_shw
    gssh_free = [None, None]
    hvsh_free = [None] * NSH
    oacc_prev_sh = T["moe_done"]
    for th in range(2):
        for i in range(NSH):
            gw = s_sg1 if i < 4 else s_sg2
            uw = s_su1 if i < 4 else s_su2
            tgw = t_sg1 if i < 4 else t_sg2
            tuw = t_su1 if i < 4 else t_su2
            ii = i % 4
            for sl2 in range(NSLH):
                col0 = TH * th + 512 * sl2
                gb = sl2 % 2
                ub = 2 + sl2 % 2
                ring = (i + sl2) % 2
                p.wait("tensor", tgw)
                p.wait("tensor", gu_bank_free[gb])
                mm = None
                for c in range(NCH):
                    mm = p.op("tensor", lambda t, c=c, ii=ii, col0=col0, gb=gb, gw=gw: t.matmul(
                        PA[:, 512 * gb:512 * (gb + 1)],
                        gw[:, 512 * c + 128 * ii:512 * c + 128 * (ii + 1)],
                        xn_own[c][:, col0:col0 + 512],
                        start=(c == 0), stop=(c == NCH - 1)),
                        sem="pe" if c == NCH - 1 else None)
                p.wait("scalar", mm)
                p.wait("scalar", gssh_free[ring])
                tsil = p.op("scalar", lambda s, gb=gb, ring=ring: s.activation(
                    out=s_gssh[ring][:], in_=PA[:, 512 * gb:512 * (gb + 1)],
                    func=AF.Silu), sem="act")
                gu_bank_free[gb] = tsil
                p.wait("tensor", tuw)
                p.wait("tensor", gu_bank_free[ub])
                mm = None
                for c in range(NCH):
                    mm = p.op("tensor", lambda t, c=c, ii=ii, col0=col0, ub=ub, uw=uw: t.matmul(
                        PA[:, 512 * ub:512 * (ub + 1)],
                        uw[:, 512 * c + 128 * ii:512 * c + 128 * (ii + 1)],
                        xn_own[c][:, col0:col0 + 512],
                        start=(c == 0), stop=(c == NCH - 1)),
                        sem="pe" if c == NCH - 1 else None)
                p.wait("vector", mm)
                p.wait("vector", tsil)
                p.wait("vector", hvsh_free[i])
                thv = p.op("vector", lambda v, i=i, ub=ub, ring=ring, sl2=sl2: v.scalar_tensor_tensor(
                    out=s_hvsh[i][:, 512 * sl2:512 * (sl2 + 1)],
                    in0=PA[:, 512 * ub:512 * (ub + 1)], scalar=1.0,
                    in1=s_gssh[ring][:], op0=ALU.mult, op1=ALU.mult), sem="dve")
                gu_bank_free[ub] = thv
                gssh_free[ring] = thv
                hvsh_free[i] = thv
        t_hv_half = list(hvsh_free)
        for sl2 in range(NSLH):
            for oc in range(NCH):
                col0 = TH * th + 512 * sl2
                bank = (sl2 * NCH + oc) % 2
                dw_tok = t_sd1 if oc < 4 else t_sd2
                dwbuf = s_sd1 if oc < 4 else s_sd2
                oc2 = oc % 4
                p.wait("tensor", dw_tok)
                p.wait("tensor", dn_bank_free[bank])
                p.wait("tensor", t_hv_half)
                mm = None
                for i in range(NSH):
                    mm = p.op("tensor", lambda t, i=i, oc2=oc2, sl2=sl2, bank=bank, dwbuf=dwbuf: t.matmul(
                        PB[:, 512 * bank:512 * (bank + 1)],
                        dwbuf[:, 512 * i + 128 * oc2:512 * i + 128 * (oc2 + 1)],
                        s_hvsh[i][:, 512 * sl2:512 * (sl2 + 1)],
                        start=(i == 0), stop=(i == NSH - 1)),
                        sem="pe" if i == NSH - 1 else None)
                p.wait("vector", mm)
                p.wait("vector", oacc_prev_sh)
                ta = p.op("vector", lambda v, oc=oc, col0=col0, bank=bank: v.scalar_tensor_tensor(
                    out=o_acc[oc][:, col0:col0 + 512],
                    in0=PB[:, 512 * bank:512 * (bank + 1)], scalar=1.0,
                    in1=o_acc[oc][:, col0:col0 + 512], op0=ALU.mult, op1=ALU.add),
                    sem="dve")
                dn_bank_free[bank] = ta
                if oc == NCH - 1:
                    T.setdefault("sh_done_sl", {})[(th, sl2)] = ta
        hv_done = ("pe", p.counts["pe"])
        for i in range(NSH):
            hvsh_free[i] = hv_done
        oacc_prev_sh = ta
    t_shd = T["shared_done"] = oacc_prev_sh
    sh_done_sl = dict(T["sh_done_sl"])
    for oc in range(NCH):
        for sl2k, tok in list(sh_done_sl.items()):
            pass
    for (th2, sl22), tok in sorted(T["sh_done_sl"].items()):
        col0 = TH * th2 + 512 * sl22
        tb_last = None
        for oc in range(NCH):
            p.wait("vector", tok)
            tb_last = p.op("vector", lambda v, oc=oc, col0=col0: v.tensor_scalar(
                out=o_acc[oc][:, col0:col0 + 512], in0=o_acc[oc][:, col0:col0 + 512],
                scalar1=s_bofm[:, oc:oc + 1], scalar2=None, op0=ALU.add), sem="dve")
        sh_done_sl[(th2, sl22)] = tb_last
    t_sh_pe = ("pe", p.counts["pe"])

    slot_free = [ln_xring_last[0], ln_xring_last[1]]
    or_free = [t_attn_pe, t_attn_pe]
    ptf_free = [None, None]
    nring = 4 if TOWN >= 2048 else 2
    or_free = [t_attn_pe, t_attn_pe, t_sh_pe, t_sh_pe][:nring]
    for t in range(NT_OWN):
        sl = t % 2
        xr = t % 2
        orr = t % nring
        p.wait("sync", slot_free[sl])
        tok_ld = p.op("sync", lambda s, t=t, sl=sl: s.dma_start(
            out=x_ring[sl][:], in_=x_seq[128 * t:128 * (t + 1), :]), sem=f"dx{sl}", dma=True)
        th_t, sl2_t = (t * 128) // TH, ((t * 128) % TH) // 512
        p.wait("tensor", sh_done_sl[(th_t, sl2_t)])
        p.wait("tensor", ptf_free[xr])
        tt = None
        for c in range(NCH):
            tt = p.op("tensor", lambda tr, t=t, c=c, xr=xr: tr.transpose(
                PT[:, 1024 * xr + 128 * c:1024 * xr + 128 * (c + 1)],
                o_acc[c][:, 128 * t:128 * (t + 1)], s_identb[:]),
                sem="pe" if c == NCH - 1 else None)
        p.wait("vector", tt)
        p.wait("vector", tok_ld)
        p.wait("vector", or_free[orr])
        tadd = p.op("vector", lambda v, sl=sl, xr=xr, orr=orr: v.scalar_tensor_tensor(
            out=out_ring[orr][:], in0=PT[:, 1024 * xr:1024 * (xr + 1)], scalar=1.0,
            in1=x_ring[sl][:], op0=ALU.mult, op1=ALU.add), sem="dve")
        ptf_free[xr] = tadd
        slot_free[sl] = tadd
        p.wait("sync", tadd)
        tst = p.op("sync", lambda s, t=t, orr=orr: s.dma_start(
            out=out_d[128 * t:128 * (t + 1), :], in_=out_ring[orr][:]), sem=f"dst{orr}", dma=True)
        or_free[orr] = tst
    for k in range(4):
        if f"dst{k}" in p.counts:
            p.wait("sync", (f"dst{k}", p.counts[f"dst{k}"]))

    p.check()
    p.emit()
    return nc


def _consts(core_parity):
    ident_bf = np.eye(128, dtype=ml_dtypes.bfloat16)
    ident_f = np.eye(128, dtype=np.float32)
    kk, qq = np.meshgrid(np.arange(128), np.arange(128), indexing="ij")
    tri = (kk <= qq).astype(ml_dtypes.bfloat16)
    band_other = np.full((128, 128), 0.0 if core_parity == 0 else 1.0,
                         dtype=ml_dtypes.bfloat16)
    tilt = np.broadcast_to((31 - np.arange(32)).astype(np.float32) * TILT,
                           (128, 32)).copy()
    sel = np.zeros((32, 32, 128), dtype=ml_dtypes.bfloat16)
    for e in range(32):
        sel[e, e, :] = 1.0
    ones = np.ones((128, 128), dtype=np.float32)
    return dict(ident_bf=ident_bf, ident_f=ident_f, tri_bf=tri,
                band_other=band_other, tilt=tilt, selmat=sel, ones_f=ones)


_NC_CACHE = {}


def _get_nc(seq, heads):
    key = (seq, heads)
    if key not in _NC_CACHE:
        _NC_CACHE[key] = build_nc(dict(seq=seq, heads=heads))
    return _NC_CACHE[key]


def make_in_maps(x, wq, wk, wv, wo, bo, rw, rb, gate, up, down, sg, su, sd,
                 n_cores):
    B, S, DM = x.shape
    f32 = np.float32
    base = dict(
        wq=np.ascontiguousarray(wq, f32), wk=np.ascontiguousarray(wk, f32),
        wv=np.ascontiguousarray(wv, f32), wo=np.ascontiguousarray(wo, f32),
        bo=np.ascontiguousarray(np.asarray(bo, f32).reshape(1, DM)),
        bo_fm=np.ascontiguousarray(np.asarray(bo, f32).reshape(8, 128).T),
        rwT=np.ascontiguousarray(np.asarray(rw, f32).T),
        rbias=np.ascontiguousarray(np.asarray(rb, f32).reshape(1, 32)),
        gateT=np.ascontiguousarray(np.asarray(gate, f32).transpose(0, 2, 1)),
        upT=np.ascontiguousarray(np.asarray(up, f32).transpose(0, 2, 1)),
        downT=np.ascontiguousarray(np.asarray(down, f32).transpose(0, 2, 1)),
        sgT=np.ascontiguousarray(np.asarray(sg, f32).T),
        suT=np.ascontiguousarray(np.asarray(su, f32).T),
        sdT=np.ascontiguousarray(np.asarray(sd, f32).T),
    )
    in_maps = []
    for core in range(n_cores):
        n, j = core // 2, core % 2
        xt = np.asarray(x[n], f32).reshape(S // 128, 128, DM)
        x_seq = np.ascontiguousarray(
            np.concatenate([xt[j::2], xt[1 - j::2]], axis=0).reshape(S, DM))
        m = dict(base)
        m["x_seq"] = x_seq
        m.update(_consts(j))
        in_maps.append(m)
    return in_maps


def assemble(results, B, S, DM):
    out = np.empty((B, S, DM), dtype=np.float32)
    for core in range(2 * B):
        n, j = core // 2, core % 2
        o = np.asarray(results[core]["out"], dtype=np.float32)
        out[n].reshape(S // 128, 128, DM)[j::2] = o.reshape(S // 256, 128, DM)
    return out


def kernel(L0_attn_x, L0_attn_Wq, L0_attn_Wk, L0_attn_Wv, L0_attn_Wo, L0_attn_bo,
           L0_ffn_router_weight, L0_ffn_router_bias, L0_ffn_gate_w, L0_ffn_up_w,
           L0_ffn_down_w, L0_ffn_shared_gate_w, L0_ffn_shared_up_w,
           L0_ffn_shared_down_w):
    x = np.ascontiguousarray(np.asarray(L0_attn_x, dtype=np.float32))
    B, S, DM = x.shape
    nc = _get_nc(S, 16)
    in_maps = make_in_maps(x, L0_attn_Wq, L0_attn_Wk, L0_attn_Wv, L0_attn_Wo,
                           L0_attn_bo, L0_ffn_router_weight, L0_ffn_router_bias,
                           L0_ffn_gate_w, L0_ffn_up_w, L0_ffn_down_w,
                           L0_ffn_shared_gate_w, L0_ffn_shared_up_w,
                           L0_ffn_shared_down_w, 2 * B)
    res = run_bass_kernel_spmd(nc, in_maps, core_ids=list(range(2 * B)))
    return assemble(res.results, B, S, DM)






# version 22
# speedup vs baseline: 1.0094x; 1.0094x over previous
import numpy as np
import ml_dtypes
import concourse.bass as bass
import concourse.mybir as mybir
from concourse.bass_utils import run_bass_kernel_spmd

F32 = mybir.dt.float32
BF16 = mybir.dt.bfloat16
AF = mybir.ActivationFunctionType
ALU = mybir.AluOpType
AX = mybir.AxisListType

ENGINES = ("sync", "tensor", "scalar", "vector", "gpsimd")
NEG = -1.0e9
TILT = 1e-6


class Plan:
    def __init__(self, nc):
        self.nc = nc
        self.items = {e: [] for e in ENGINES}
        self.sems = {}
        self.counts = {}

    def _sem(self, name):
        if name not in self.sems:
            self.sems[name] = self.nc.semaphore(name).__enter__()
            self.counts[name] = 0

    def op(self, engine, fn, sem=None, dma=False):
        it = ["op", fn, None, dma]
        self.items[engine].append(it)
        if sem is None:
            return None
        self._sem(sem)
        amt = 16 if dma else 1
        self.counts[sem] += amt
        it[2] = (sem, amt)
        return (sem, self.counts[sem])

    def nxt(self, sem):
        self._sem(sem)
        return (sem, self.counts[sem] + 1)

    def wait(self, engine, tok):
        if tok is None:
            return
        if isinstance(tok, (list, tuple)) and tok and isinstance(tok[0], (list, tuple)):
            for t in tok:
                self.wait(engine, t)
            return
        self.items[engine].append(["wait", tok[0], tok[1]])

    def check(self):
        from collections import deque
        ptr = {e: 0 for e in ENGINES}
        semv = {s: 0 for s in self.counts}
        queues = {e: deque() for e in ENGINES}
        progress = True
        while progress:
            progress = False
            for e in ENGINES:
                while ptr[e] < len(self.items[e]):
                    it = self.items[e][ptr[e]]
                    if it[0] == "wait":
                        if semv.get(it[1], 0) >= it[2]:
                            ptr[e] += 1
                            progress = True
                        else:
                            break
                    else:
                        if it[3]:
                            queues[e].append(it[2])
                        elif it[2] is not None:
                            semv[it[2][0]] += it[2][1]
                        ptr[e] += 1
                        progress = True
            for q in queues.values():
                while q:
                    inc = q.popleft()
                    if inc is not None:
                        semv[inc[0]] += inc[1]
                    progress = True
        stuck = {e: ptr[e] for e in ENGINES if ptr[e] < len(self.items[e])}
        if stuck:
            msg = []
            for e, pp in stuck.items():
                it = self.items[e][pp]
                msg.append(f"{e}@{pp}/{len(self.items[e])}: wait {it[1]}>={it[2]}"
                           f" (now {semv.get(it[1], 0)})")
            raise RuntimeError("Plan deadlock: " + "; ".join(msg))
        for s, v in self.counts.items():
            assert semv.get(s, 0) == v, (s, semv.get(s, 0), v)

    def emit(self):
        with self.nc.Block() as block:
            for ename in ENGINES:
                def make_body(items=self.items[ename], sems=self.sems):
                    def body(eng):
                        maxw = {}
                        for it in items:
                            if it[0] == "wait":
                                if maxw.get(it[1], -1) >= it[2]:
                                    continue
                                eng.wait_ge(sems[it[1]], it[2])
                                maxw[it[1]] = it[2]
                            else:
                                inst = it[1](eng)
                                if it[2] is not None:
                                    inst.then_inc(sems[it[2][0]], it[2][1])
                    return body
                getattr(block, ename)(make_body())


def build_nc(cfg):
    SEQ = cfg["seq"]
    TOWN = SEQ // 2
    HEADS = cfg["heads"]
    DM = 1024
    NCH = 8
    NT_OWN = TOWN // 128
    NT_ALL = SEQ // 128
    NQS = TOWN // 512
    NPAIR = HEADS // 2
    E = 32
    IMOE = 512
    NI = 4
    SHI = 1024
    NSH = 8
    TH = TOWN // 2
    NSL = TOWN // 512
    NSLH = TH // 512
    SCALE = 0.125

    nc = bass.Bass("TRN2", target_bir_lowering=False, debug=False)

    x_seq = nc.dram_tensor("x_seq", [SEQ, DM], F32, kind="ExternalInput").ap()
    wq_d = nc.dram_tensor("wq", [DM, 128 * NPAIR], F32, kind="ExternalInput").ap()
    wk_d = nc.dram_tensor("wk", [DM, 128 * NPAIR], F32, kind="ExternalInput").ap()
    wv_d = nc.dram_tensor("wv", [DM, 128 * NPAIR], F32, kind="ExternalInput").ap()
    wo_d = nc.dram_tensor("wo", [128 * NPAIR, DM], F32, kind="ExternalInput").ap()
    bo_d = nc.dram_tensor("bo", [1, DM], F32, kind="ExternalInput").ap()
    rw_d = nc.dram_tensor("rwT", [DM, E], F32, kind="ExternalInput").ap()
    rb_d = nc.dram_tensor("rbias", [1, E], F32, kind="ExternalInput").ap()
    gate_d = nc.dram_tensor("gateT", [E, DM, IMOE], F32, kind="ExternalInput").ap()
    up_d = nc.dram_tensor("upT", [E, DM, IMOE], F32, kind="ExternalInput").ap()
    down_d = nc.dram_tensor("downT", [E, IMOE, DM], F32, kind="ExternalInput").ap()
    sg_d = nc.dram_tensor("sgT", [DM, SHI], F32, kind="ExternalInput").ap()
    su_d = nc.dram_tensor("suT", [DM, SHI], F32, kind="ExternalInput").ap()
    sd_d = nc.dram_tensor("sdT", [SHI, DM], F32, kind="ExternalInput").ap()
    identb_d = nc.dram_tensor("ident_bf", [128, 128], BF16, kind="ExternalInput").ap()
    identf_d = nc.dram_tensor("ident_f", [128, 128], F32, kind="ExternalInput").ap()
    tri_d = nc.dram_tensor("tri_bf", [128, 128], BF16, kind="ExternalInput").ap()
    bando_d = nc.dram_tensor("band_other", [128, 128], BF16, kind="ExternalInput").ap()
    tilt_d = nc.dram_tensor("tilt", [128, E], F32, kind="ExternalInput").ap()
    sel_d = nc.dram_tensor("selmat", [E, E, 128], BF16, kind="ExternalInput").ap()
    ones_d = nc.dram_tensor("ones_f", [128, 128], F32, kind="ExternalInput").ap()
    bofm_d = nc.dram_tensor("bo_fm", [128, NCH], F32, kind="ExternalInput").ap()
    out_d = nc.dram_tensor("out", [TOWN, DM], F32, kind="ExternalOutput").ap()

    A = nc.alloc_sbuf_tensor
    xn_own_t = A("xn_own_t", [128, NCH * TOWN], BF16)
    xn_own = [xn_own_t[:, TOWN * c:TOWN * (c + 1)] for c in range(NCH)]
    o_acc = [A(f"o_acc{c}", [128, TOWN], BF16) for c in range(NCH)]
    s_identb = A("s_identb", [128, 128], BF16)
    s_identf = A("s_identf", [128, 128], F32)
    s_tri = A("s_tri", [128, 128], BF16)
    s_bando = A("s_bando", [128, 128], BF16)
    s_tilt = A("s_tilt", [128, E], F32)
    s_ones = A("s_ones", [128, 128], F32)
    s_onescol = A("s_onescol", [128, 1], BF16)
    s_onesb = A("s_onesb", [128, 64], BF16)
    s_bo_bc = A("s_bo_bc", [128, DM], F32)
    s_rb_bc = A("s_rb_bc", [128, E], F32)
    s_wfm = A("s_wfm", [E, TOWN], BF16)
    pbuf = [A(f"pbuf{i}", [128, 1024], BF16) for i in range(6)]
    x_ring = [A(f"x_ring{i}", [128, DM], F32) for i in range(2)]
    xn_tm = [A(f"xn_tm{i}", [128, DM], BF16) for i in range(2)]
    s_sum = A("s_sum", [128, NT_ALL], F32)
    s_sumsq = A("s_sumsq", [128, NT_ALL], F32)
    s_mean = A("s_mean", [128, NT_ALL], F32)
    s_rstd = A("s_rstd", [128, NT_ALL], F32)
    s_m2 = A("s_m2", [128, NT_ALL], F32)
    s_rrow = A("s_rrow", [128, 1024], BF16)
    s_batt = A("s_batt", [64, 1024], F32)
    s_pvout = [A(f"s_pvout{h}", [65, 512], F32) for h in range(2)]
    s_gm1 = A("s_gm1", [128, NT_OWN * 4], F32)
    s_gm2 = A("s_gm2", [128, NT_OWN * 4], F32)
    s_gs = A("s_gs", [128, NT_OWN * 4], F32)
    s_gred = A("s_gred", [128, NT_OWN], F32)
    s_rw = A("s_rw", [128, NCH * E], BF16)
    s_bofm = A("s_bofm", [128, NCH], F32)
    s_topm = A("s_topm", [128, NT_OWN], F32)
    s_den = A("s_den", [128, NT_OWN], F32)

    used = nc.SBUF_PARTITION_SIZE_BYTES - nc.sbuf_bytes_remaining
    base = (used + 95) // 32 * 32
    cap = 229344

    def at(name, shape, dtype, off):
        nbytes = {F32: 4, BF16: 2}[dtype] * shape[-1]
        assert base + off + nbytes <= cap, (name, base, off, nbytes)
        return nc.alloc_sbuf_tensor_at(name, shape, dtype, offset=base + off)

    xn_other_t = at("xn_other_t", [128, NCH * TOWN], BF16, 0)
    xn_other = [xn_other_t[:, TOWN * c:TOWN * (c + 1)] for c in range(NCH)]
    o1 = 2 * TOWN * NCH
    s_wq = [at(f"s_wq{i}", [128, DM], BF16, o1 + 2048 * i) for i in range(2)]
    s_wk = [at(f"s_wk{i}", [128, DM], BF16, o1 + 4096 + 2048 * i) for i in range(2)]
    s_wv = [at(f"s_wv{i}", [128, DM], BF16, o1 + 8192 + 2048 * i) for i in range(2)]
    o2 = o1 + 12288
    s_kp = at("s_kp", [128, SEQ], BF16, o2)
    o3 = o2 + 2 * SEQ
    s_qp = at("s_qp", [128, TOWN], BF16, o3)
    o4 = o3 + 2 * TOWN
    VSTR = NT_ALL * 65 + 16
    s_vaug_t = at("s_vaug_t", [128, 2 * VSTR], BF16, o4)
    s_vaug = [s_vaug_t[:, VSTR * h:VSTR * h + NT_ALL * 65] for h in range(2)]
    o5 = o4 + 2 * (2 * VSTR)
    s_vtmp = [at(f"s_vtmp{i}", [128, 512], BF16, o5 + 1024 * i) for i in range(2)]
    o6 = o5 + 2048
    s_apair = at("s_apair", [128, TOWN], BF16, o6)
    s_junk = at("s_junk", [128, DM], F32, o6)
    out_ring = [at("out_ring0", [128, DM], F32, o6)]
    o7 = o6 + max(2 * TOWN, 4096)
    s_apairB = at("s_apairB", [64, TOWN], BF16, o7)
    x_ring.append(at("x_ring2", [128, DM], F32, o7))
    o8 = o7 + max(2 * TOWN, 4096)
    out_ring.append(at("out_ring1", [128, DM], F32, o8))
    out_ring.append(at("out_ring2", [128, DM], F32, 49152))
    out_ring.append(at("out_ring3", [128, DM], F32, 49152 + 4096))
    s_wo = at("s_wo", [128, NPAIR * DM], BF16, o8)
    o9 = o8 + 2 * NPAIR * DM
    assert base + o9 <= cap, (base, o9)

    s_gate = [at(f"s_gate{i}", [128, NCH * IMOE], BF16, 8192 * i) for i in range(2)]
    s_up = [at(f"s_up{i}", [128, NCH * IMOE], BF16, 16384 + 8192 * i) for i in range(2)]
    s_down = [at(f"s_down{i}", [128, NI * DM], BF16, 32768 + 8192 * i) for i in range(2)]
    s_be = [at(f"s_be{i}", [128, TOWN], BF16, 49152 + 2 * TOWN * i) for i in range(2)]
    m1 = 49152 + 4 * TOWN
    s_hv = [at(f"s_hv{i}", [128, TOWN], BF16, m1 + 2 * TOWN * i) for i in range(NI)]
    m2 = m1 + 2 * TOWN * NI
    s_gsb = [at(f"s_gsb{i}", [128, 512], BF16, m2 + 1024 * i) for i in range(2)]
    s_usb = [at(f"s_usb{i}", [128, 512], BF16, m2 + 2048 + 1024 * i) for i in range(2)]
    m3 = m2 + 4096
    sz = NT_OWN * E * 4
    s_scores = at("s_scores", [128, NT_OWN * E], F32, m3)
    s_sfc = at("s_sfc", [128, NT_OWN * E], F32, m3 + sz)
    s_tmp = at("s_tmp", [128, NT_OWN * E], F32, m3 + 2 * sz)
    s_selm = at("s_selm", [128, NT_OWN * E], F32, m3 + 3 * sz)
    s_ism = at("s_ism", [128, NT_OWN * E], F32, m3 + 4 * sz)
    s_wtm = at("s_wtm", [128, NT_OWN * E], F32, m3 + 5 * sz)
    m4 = m3 + 6 * sz
    s_sel = at("s_sel", [E, E * 128], BF16, m4)
    m5 = m4 + E * 128 * 2
    assert base + m5 <= cap, (base, m5)

    s_sg1 = at("s_sg1", [128, NCH * 512], BF16, 0)
    s_sg2 = at("s_sg2", [128, NCH * 512], BF16, 8192)
    s_su1 = at("s_su1", [128, NCH * 512], BF16, 16384)
    s_su2 = at("s_su2", [128, NCH * 512], BF16, 24576)
    s_sd1 = at("s_sd1", [128, NSH * 512], BF16, 32768)
    s_sd2 = at("s_sd2", [128, NSH * 512], BF16, 40960)
    s_hvsh = [at(f"s_hvsh{i}", [128, TH], BF16, 49152 + 2 * TH * i) for i in range(NSH)]
    sh1 = 49152 + 2 * TH * NSH
    s_gssh = [at(f"s_gssh{i}", [128, 512], BF16, sh1 + 1024 * i) for i in range(2)]
    assert base + sh1 + 2048 <= cap

    PA = nc.alloc_psum_tensor("PA", [128, 2048], F32)
    PB = nc.alloc_psum_tensor("PB", [128, 1024], F32)
    PT = nc.alloc_psum_tensor("PT", [128, 2048], BF16, side="right")

    p = Plan(nc)
    T = {}

    for src, dst in ((identb_d, s_identb), (identf_d, s_identf), (tri_d, s_tri),
                     (bando_d, s_bando), (tilt_d, s_tilt), (ones_d, s_ones)):
        p.op("sync", lambda s, a=src, b=dst: s.dma_start(out=b[:], in_=a), sem="dc", dma=True)
    p.op("sync", lambda s: s.dma_start(out=s_junk[0:1, 0:DM], in_=bo_d), sem="dc", dma=True)
    p.op("sync", lambda s: s.dma_start(out=s_bofm[:], in_=bofm_d), sem="dc", dma=True)
    t_consts = p.op("sync", lambda s: s.dma_start(out=s_junk[32:33, 0:E], in_=rb_d),
                  sem="dc", dma=True)
    p.wait("vector", t_consts)
    p.wait("scalar", t_consts)
    t_wo = p.op("gpsimd", lambda g: g.dma_start(
        out=s_wo[:].rearrange("p (r n) -> p r n", r=NPAIR),
        in_=wo_d.rearrange("(r p) n -> p r n", p=128)), sem="dwo", dma=True)
    t_rw = p.op("gpsimd", lambda g: g.dma_start(
        out=s_rw[:].rearrange("p (c e) -> p c e", c=NCH),
        in_=rw_d.rearrange("(c p) e -> p c e", p=128)), sem="drw", dma=True)
    t_onescol = p.op("gpsimd", lambda g: g.memset(s_onescol[:], 1.0), sem="pool")
    t_onesb = p.op("gpsimd", lambda g: g.memset(s_onesb[:], 1.0), sem="pool")
    pz_pbuf = [p.op("gpsimd", lambda g, b=b: g.memset(b[:], 0.0), sem="pool")
               for b in pbuf]
    pz_vaug = [p.op("gpsimd", lambda g: g.memset(s_vaug_t[:], 1.0), sem="pool")] * 2

    p.wait("tensor", t_consts)
    p.op("tensor", lambda t: t.matmul(PA[:, 0:512], s_ones[0:1, :], s_junk[0:1, 0:512],
                                      start=True, stop=True))
    p.op("tensor", lambda t: t.matmul(PA[:, 512:1024], s_ones[0:1, :],
                                      s_junk[0:1, 512:1024], start=True, stop=True))
    t_bcmm = p.op("tensor", lambda t: t.matmul(PA[:, DM:DM + E], s_ones[32:33, :],
                                               s_junk[32:33, 0:E],
                                               start=True, stop=True,
                                               tile_position=(32, 0)), sem="pe")
    p.wait("scalar", t_bcmm)
    p.op("scalar", lambda s: s.copy(out=s_bo_bc[:], in_=PA[:, 0:DM]))
    t_bc = p.op("scalar", lambda s: s.copy(out=s_rb_bc[:], in_=PA[:, DM:DM + E]),
                sem="act")

    slot_free = [None, t_bcmm, None]
    t_sq_prev = t_bcmm
    xtm_free = [None, None]
    pt_free = [None, None]
    ln_evac = [None] * NT_ALL
    ln_xring_last = [None, None, None]
    st_a = {}
    for tt_ in range(NT_ALL + 2):
        if tt_ < NT_ALL:
            t = tt_
            sl = t % 3
            p.wait("sync", slot_free[sl])
            tok_ld = p.op("sync", lambda s, t=t, sl=sl: s.dma_start(
                out=x_ring[sl][:], in_=x_seq[128 * t:128 * (t + 1), :]),
                sem=f"dx{sl}", dma=True)
            p.wait("scalar", tok_ld)
            p.wait("scalar", t_sq_prev)
            ts0 = p.op("scalar", lambda s, t=t, sl=sl: s.activation(
                out=s_junk[:], in_=x_ring[sl][:], func=AF.Square,
                accum_out=s_sumsq[:, t:t + 1]), sem="act")
            p.wait("scalar", ts0)
            ts_ = p.op("scalar", lambda s, t=t, sl=sl: s.activation(
                out=s_junk[:], in_=x_ring[sl][:], func=AF.Copy,
                accum_out=s_sum[:, t:t + 1]), sem="act")
            t_sq_prev = ts_
            st_a[t] = [tok_ld, ts_, ts_]
        if 1 <= tt_ and tt_ - 1 < NT_ALL:
            t = tt_ - 1
            p.wait("scalar", st_a[t][1])
            c1 = p.op("scalar", lambda s, t=t: s.mul(
                out=s_mean[:, t:t + 1], in_=s_sum[:, t:t + 1], mul=1.0 / DM), sem="act")
            p.wait("scalar", c1)
            c2 = p.op("scalar", lambda s, t=t: s.activation(
                out=s_m2[:, t:t + 1], in_=s_mean[:, t:t + 1], func=AF.Square), sem="act")
            p.wait("scalar", c2)
            c3 = p.op("scalar", lambda s, t=t: s.activation(
                out=s_m2[:, t:t + 1], in_=s_m2[:, t:t + 1], func=AF.Copy,
                bias=1e-5, scale=-1.0), sem="act")
            p.wait("scalar", c3)
            c4 = p.op("scalar", lambda s, t=t: s.activation(
                out=s_rstd[:, t:t + 1], in_=s_sumsq[:, t:t + 1], func=AF.Relu,
                bias=s_m2[:, t:t + 1], scale=1.0 / DM), sem="act")
            p.wait("scalar", c4)
            s1 = p.op("scalar", lambda s, t=t: s.activation(
                out=s_rstd[:, t:t + 1], in_=s_rstd[:, t:t + 1], func=AF.Sqrt), sem="act")
            st_a[t].append(s1)
        if tt_ >= 2:
            t = tt_ - 2
            sl = t % 3
            xr = t % 2
            s1 = st_a[t][3]
            p.wait("vector", s1)
            c5 = p.op("vector", lambda v, t=t: v.reciprocal(
                s_rstd[:, t:t + 1], s_rstd[:, t:t + 1]), sem="dve")
            p.wait("vector", c5)
            p.wait("vector", xtm_free[xr])
            tn = p.op("vector", lambda v, t=t, sl=sl, xr=xr: v.tensor_scalar(
                out=xn_tm[xr][:], in0=x_ring[sl][:], scalar1=s_mean[:, t:t + 1],
                scalar2=s_rstd[:, t:t + 1], op0=ALU.subtract, op1=ALU.mult), sem="dve")
            slot_free[sl] = tn
            ln_xring_last[sl] = tn
            p.wait("tensor", tn)
            p.wait("tensor", pt_free[xr])
            ttr = None
            for c in range(NCH):
                ttr = p.op("tensor", lambda tr, xr=xr, c=c: tr.transpose(
                    PT[:, 1024 * xr + 128 * c:1024 * xr + 128 * (c + 1)],
                    xn_tm[xr][:, 128 * c:128 * (c + 1)], s_identb[:]),
                    sem="pe" if c == NCH - 1 else None)
            xtm_free[xr] = ttr
            dstt = xn_own_t if t < NT_OWN else xn_other_t
            tloc = t if t < NT_OWN else t - NT_OWN
            p.wait("scalar", ttr)
            ev = p.op("scalar", lambda s, xr=xr, dstt=dstt, tloc=tloc: s.copy(
                out=dstt[:].rearrange("p (c n) -> p c n", c=NCH)[:, :, 128 * tloc:128 * (tloc + 1)],
                in_=PT[:, 1024 * xr:1024 * (xr + 1)].rearrange("p (c n) -> p c n", c=NCH)),
                sem="act")
            pt_free[xr] = ev
            ln_evac[t] = ev
    T["ln_done"] = ln_evac[NT_ALL - 1]

    def xn_col(chunk, col0, n):
        if col0 + n <= TOWN:
            return xn_own[chunk][:, col0:col0 + n]
        assert col0 >= TOWN
        return xn_other[chunk][:, col0 - TOWN:col0 - TOWN + n]

    wbuf_free = [None, None]
    kvq_free = None
    sband_free = [None, None]
    pvgate = [None, None]
    pbuf_free = {i: pz_pbuf[i] for i in range(6)}
    vaug_first = {0: pz_vaug[0], 1: pz_vaug[1]}
    pt_kv_free = [None, None]
    vtmp_free = [None, None]
    oacc_prev = None
    pvout_free = [None, None]
    rrow_free = [None, None]
    batt_free = [None, None]
    wo_prev_pe = None
    apB_prev = [None, None, None, None]
    wo_state = {}

    def emit_wo(pair_w, t_apB):
        nonlocal oacc_prev, sband_free
        p.wait("tensor", wo_state["t_wo_dma"])
        wo_state["t_wo_dma"] = None
        wo_bank_free = wo_state.get("wo_bank_free", [None, None])
        ta = None
        tw = None
        for oc in range(NCH):
            for sl2 in range(NSL):
                bank = (oc * NSL + sl2) % 2
                p.wait("tensor", wo_bank_free[bank])
                p.wait("tensor", wo_state["last_tat"])
                p.wait("tensor", t_apB)
                p.wait("tensor", sband_free[0])
                tw = p.op("tensor", lambda t, pair=pair_w, oc=oc, sl2=sl2, bank=bank: t.matmul(
                    PA[:, 512 * bank:512 * (bank + 1)],
                    s_wo[:, DM * pair + 128 * oc:DM * pair + 128 * (oc + 1)],
                    s_apair[:, 512 * sl2:512 * (sl2 + 1)],
                    start=True, stop=True), sem="pe")
                p.wait("vector", tw)
                p.wait("vector", oacc_prev)
                if pair_w == 0:
                    ta = p.op("vector", lambda v, oc=oc, sl2=sl2, bank=bank: v.tensor_copy(
                        out=o_acc[oc][:, 512 * sl2:512 * (sl2 + 1)],
                        in_=PA[:, 512 * bank:512 * (bank + 1)]), sem="dve")
                else:
                    ta = p.op("vector", lambda v, oc=oc, sl2=sl2, bank=bank: v.scalar_tensor_tensor(
                        out=o_acc[oc][:, 512 * sl2:512 * (sl2 + 1)],
                        in0=PA[:, 512 * bank:512 * (bank + 1)], scalar=1.0,
                        in1=o_acc[oc][:, 512 * sl2:512 * (sl2 + 1)],
                        op0=ALU.mult, op1=ALU.add), sem="dve")
                wo_bank_free[bank] = ta
        oacc_prev = ta
        wo_state["wo_bank_free"] = wo_bank_free
        sband_free[0] = ta
        return tw

    wo_state["t_wo_dma"] = t_wo
    for pair in range(NPAIR):
        ws = pair % 2
        for dram, buf in ((wq_d, s_wq[ws]), (wk_d, s_wk[ws]), (wv_d, s_wv[ws])):
            p.wait("gpsimd", wbuf_free[ws])
            p.op("gpsimd", lambda g, dram=dram, buf=buf, pair=pair: g.dma_start(
                out=buf[:].rearrange("p (c n) -> p c n", c=NCH),
                in_=dram.rearrange("(c p) n -> p c n", p=128)[:, :, 128 * pair:128 * (pair + 1)]),
                sem=f"dwa{ws}", dma=True)
        t_w = (f"dwa{ws}", p.counts[f"dwa{ws}"])

        p.wait("tensor", t_w)
        p.wait("tensor", kvq_free)
        p.wait("tensor", sband_free[0])
        kv_bank_free = [None, None]
        for tsl in range(SEQ // 512):
            bank = tsl % 2
            p.wait("tensor", ln_evac[4 * tsl + 3])
            p.wait("tensor", kv_bank_free[bank])
            mm = None
            for c in range(NCH):
                mm = p.op("tensor", lambda t, c=c, tsl=tsl, bank=bank, ws=ws: t.matmul(
                    PA[:, 512 * bank:512 * (bank + 1)],
                    s_wk[ws][:, 128 * c:128 * (c + 1)], xn_col(c, 512 * tsl, 512),
                    start=(c == 0), stop=(c == NCH - 1)),
                    sem="pe" if c == NCH - 1 else None)
            p.wait("vector", mm)
            ev = p.op("vector", lambda v, tsl=tsl, bank=bank: v.tensor_copy(
                out=s_kp[:, 512 * tsl:512 * (tsl + 1)],
                in_=PA[:, 512 * bank:512 * (bank + 1)]), sem="dve")
            kv_bank_free[bank] = ev
        vpend = None

        def flush_vtrans():
            nonlocal vpend
            if vpend is None:
                return
            vs0, tsl0, ev_v0 = vpend
            vpend = None
            p.wait("tensor", ev_v0)
            p.wait("tensor", T["ln_done"])
            p.wait("tensor", pt_kv_free[tsl0 % 2])
            ev2 = None
            for b in range(4):
                ptc = 1024 * (tsl0 % 2) + 128 * b
                tt = p.op("tensor", lambda t, vs0=vs0, b=b, ptc=ptc: t.transpose(
                    PT[:, ptc:ptc + 128], s_vtmp[vs0][:, 128 * b:128 * (b + 1)],
                    s_identb[:]), sem="pe")
                kt2 = 4 * tsl0 + b
                p.wait("vector", tt)
                p.wait("vector", vaug_first[0])
                vaug_first[0] = vaug_first[1] = None
                ev2 = p.op("vector", lambda v, kt2=kt2, ptc=ptc: v.tensor_copy(
                    out=s_vaug_t[:].rearrange("p (h n) -> p h n", h=2)[:, :, 65 * kt2:65 * kt2 + 64],
                    in_=PT[:, ptc:ptc + 128].rearrange("p (h n) -> p h n", h=2)), sem="dve")
                vtmp_free[vs0] = tt
            pt_kv_free[tsl0 % 2] = ev2

        for tsl in range(SEQ // 512):
            bank = tsl % 2
            vs = tsl % 2
            p.wait("tensor", ln_evac[4 * tsl + 3])
            p.wait("tensor", kv_bank_free[bank])
            mm = None
            for c in range(NCH):
                mm = p.op("tensor", lambda t, c=c, tsl=tsl, bank=bank, ws=ws: t.matmul(
                    PA[:, 512 * bank:512 * (bank + 1)],
                    s_wv[ws][:, 128 * c:128 * (c + 1)], xn_col(c, 512 * tsl, 512),
                    start=(c == 0), stop=(c == NCH - 1)),
                    sem="pe" if c == NCH - 1 else None)
            p.wait("vector", mm)
            p.wait("vector", vtmp_free[vs])
            ev_v = p.op("vector", lambda v, vs=vs, bank=bank: v.tensor_copy(
                out=s_vtmp[vs][:], in_=PA[:, 512 * bank:512 * (bank + 1)]), sem="dve")
            kv_bank_free[bank] = ev_v
            flush_vtrans()
            vpend = (vs, tsl, ev_v)
        flush_vtrans()
        for tsl in range(NSL):
            bank = tsl % 2
            p.wait("tensor", kv_bank_free[bank])
            mm = None
            for c in range(NCH):
                mm = p.op("tensor", lambda t, c=c, tsl=tsl, bank=bank, ws=ws: t.matmul(
                    PA[:, 512 * bank:512 * (bank + 1)],
                    s_wq[ws][:, 128 * c:128 * (c + 1)],
                    xn_own[c][:, 512 * tsl:512 * (tsl + 1)],
                    start=(c == 0), stop=(c == NCH - 1)),
                    sem="pe" if c == NCH - 1 else None)
            p.wait("vector", mm)
            ev = p.op("vector", lambda v, tsl=tsl, bank=bank: v.tensor_copy(
                out=s_qp[:, 512 * tsl:512 * (tsl + 1)],
                in_=PA[:, 512 * bank:512 * (bank + 1)]), sem="dve")
            kv_bank_free[bank] = ev
        wbuf_free[ws] = mm
        t_kvq = ("dve", p.counts["dve"])
        if pair == NPAIR - 1:
            T["kvq_last"] = ("pe", p.counts["pe"])
        sband_free[0] = kv_bank_free[1]

        if pair > 0:
            wo_prev_pe = emit_wo(pair - 1, apB_prev)

        last_tatB = None
        epi = None

        def flush_epi():
            nonlocal epi, last_tatB, rrow_free, batt_free, pvout_free, pvgate
            if epi is None:
                return
            trecA, trecB, ej = epi
            epi = None
            p.wait("tensor", trecA)
            p.wait("tensor", t_onesb)
            tbcA = p.op("tensor", lambda t: t.matmul(
                PB[0:64, 0:512], s_onesb[64:65, :], s_rrow[64:65, 0:512],
                start=True, stop=True, tile_position=(64, 0)), sem="pe")
            p.wait("tensor", trecB)
            tbcB = p.op("tensor", lambda t: t.matmul(
                PB[0:64, 512:1024], s_onesb[64:65, :], s_rrow[64:65, 512:1024],
                start=True, stop=True, tile_position=(64, 0)), sem="pe")
            rrow_free = [tbcA, tbcB]
            p.wait("scalar", tbcA)
            p.wait("scalar", batt_free[0])
            tbeA = p.op("scalar", lambda s: s.copy(out=s_batt[:, 0:512],
                                                   in_=PB[0:64, 0:512]), sem="act")
            p.wait("scalar", tbcB)
            p.wait("scalar", batt_free[1])
            tbeB = p.op("scalar", lambda s: s.copy(out=s_batt[:, 512:1024],
                                                   in_=PB[0:64, 512:1024]), sem="act")
            pvgate = [tbeA, tbeB]
            p.wait("vector", tbeA)
            if pair > 0 and ej == 0:
                p.wait("vector", wo_prev_pe)
            tatA = p.op("vector", lambda v, ej=ej: v.tensor_tensor(
                out=s_apair[0:64, 512 * ej:512 * (ej + 1)],
                in0=s_pvout[0][0:64, :], in1=s_batt[:, 0:512], op=ALU.mult), sem="dve")
            p.wait("vector", tbeB)
            p.wait("vector", apB_prev[ej])
            tatB = p.op("vector", lambda v, ej=ej: v.tensor_tensor(
                out=s_apairB[:, 512 * ej:512 * (ej + 1)],
                in0=s_pvout[1][0:64, :], in1=s_batt[:, 512:1024], op=ALU.mult), sem="dve")
            pvout_free = [tatA, tatB]
            batt_free = [tatA, tatB]
            last_tatB = tatB
            p.wait("sync", tatB)
            apB_prev[ej] = p.op("sync", lambda s, ej=ej: s.dma_start(
                out=s_apair[64:128, 512 * ej:512 * (ej + 1)],
                in_=s_apairB[:, 512 * ej:512 * (ej + 1)]), sem=f"dap{ej}", dma=True)

        for js in range(NQS):
            kts = [(0, kt) for kt in range(4 * js + 4)] + \
                  [(1, kt) for kt in range(4 * js + 4)]
            n_kt = len(kts)
            tpvA = tpvB = None
            pending = None
            for idx in range(n_kt + 1):
                if idx == 1:
                    flush_epi()
                if idx < n_kt:
                    reg, kt = kts[idx]
                    r = kt - 4 * js
                    q0 = 128 * max(r, 0)
                    bp = idx % 2
                    kcol = 128 * kt + (TOWN if reg == 1 else 0)
                    p.wait("tensor", sband_free[bp])
                    p.wait("tensor", t_kvq)
                    p.op("tensor", lambda t, kcol=kcol, js=js, q0=q0, bp=bp: t.matmul(
                        PA[:, 1024 * bp + q0:1024 * bp + 512],
                        s_kp[0:64, kcol:kcol + 128],
                        s_qp[0:64, 512 * js + q0:512 * (js + 1)],
                        start=True, stop=True, tile_position=(0, 0)))
                    tsb = p.op("tensor", lambda t, kcol=kcol, js=js, q0=q0, bp=bp: t.matmul(
                        PA[:, 1024 * bp + 512 + q0:1024 * (bp + 1)],
                        s_kp[64:128, kcol:kcol + 128],
                        s_qp[64:128, 512 * js + q0:512 * (js + 1)],
                        start=True, stop=True, tile_position=(64, 0)), sem="pe")
                    vb = idx % 2 if r < 0 else 2 + r
                    p.wait("scalar", tsb)
                    p.wait("scalar", pbuf_free[vb])
                    if r < 0:
                        texpA = p.op("scalar", lambda s, vb=vb, bp=bp: s.activation(
                            out=pbuf[vb][:, 0:512], in_=PA[:, 1024 * bp:1024 * bp + 512],
                            func=AF.Exp, scale=SCALE), sem="act")
                        texpB = p.op("scalar", lambda s, vb=vb, bp=bp: s.activation(
                            out=pbuf[vb][:, 512:1024],
                            in_=PA[:, 1024 * bp + 512:1024 * (bp + 1)],
                            func=AF.Exp, scale=SCALE), sem="act")
                    else:
                        texpA = p.op("scalar", lambda s, vb=vb, q0=q0, bp=bp: s.activation(
                            out=pbuf[vb][:, q0:512], in_=PA[:, 1024 * bp + q0:1024 * bp + 512],
                            func=AF.Exp, scale=SCALE), sem="act")
                        texpB = p.op("scalar", lambda s, vb=vb, q0=q0, bp=bp: s.activation(
                            out=pbuf[vb][:, 512 + q0:1024],
                            in_=PA[:, 1024 * bp + 512 + q0:1024 * (bp + 1)],
                            func=AF.Exp, scale=SCALE), sem="act")
                    sband_free[bp] = texpB
                    depA, depB = texpA, texpB
                    if r >= 0:
                        band = s_tri if reg == 0 else s_bando
                        p.wait("vector", texpA)
                        depA = p.op("vector", lambda v, vb=vb, q0=q0, band=band: v.tensor_tensor(
                            out=pbuf[vb][:, q0:q0 + 128], in0=pbuf[vb][:, q0:q0 + 128],
                            in1=band[:], op=ALU.mult), sem="dve")
                        p.wait("vector", texpB)
                        depB = p.op("vector", lambda v, vb=vb, q0=q0, band=band: v.tensor_tensor(
                            out=pbuf[vb][:, 512 + q0:512 + q0 + 128],
                            in0=pbuf[vb][:, 512 + q0:512 + q0 + 128],
                            in1=band[:], op=ALU.mult), sem="dve")
                    kt2 = kt + (NT_OWN if reg == 1 else 0)
                    cur = (kt2, vb, depA, depB, idx == 0, idx == n_kt - 1)
                else:
                    cur = None
                if pending is not None:
                    kt2, vb, depA, depB, first, last = pending
                    p.wait("tensor", depA)
                    if first:
                        p.wait("tensor", pvgate[0])
                    tpvA = p.op("tensor", lambda t, kt2=kt2, vb=vb, first=first, last=last: t.matmul(
                        PB[0:65, 0:512], s_vaug[0][:, 65 * kt2:65 * (kt2 + 1)],
                        pbuf[vb][:, 0:512], start=first, stop=last),
                        sem="pe" if last else None)
                    p.wait("tensor", depB)
                    if first:
                        p.wait("tensor", pvgate[1])
                    tpvB = p.op("tensor", lambda t, kt2=kt2, vb=vb, first=first, last=last: t.matmul(
                        PB[0:65, 512:1024], s_vaug[1][:, 65 * kt2:65 * (kt2 + 1)],
                        pbuf[vb][:, 512:1024], start=first, stop=last), sem="pe")
                    pbuf_free[vb] = tpvB
                pending = cur
            p.wait("scalar", tpvA)
            p.wait("scalar", tpvB)
            p.wait("scalar", pvout_free[0])
            tevA = p.op("scalar", lambda s: s.copy(out=s_pvout[0][:], in_=PB[0:65, 0:512]),
                        sem="act")
            p.wait("scalar", pvout_free[1])
            tevB = p.op("scalar", lambda s: s.copy(out=s_pvout[1][:], in_=PB[0:65, 512:1024]),
                        sem="act")
            p.wait("vector", tevA)
            p.wait("vector", rrow_free[0])
            def _recA(v):
                with nc.allow_low_precision(reason="bf16 softmax-recip broadcast"):
                    return v.reciprocal(s_rrow[64:65, 0:512], s_pvout[0][64:65, :])
            trecA = p.op("vector", _recA, sem="dve")
            p.wait("vector", tevB)
            p.wait("vector", rrow_free[1])
            def _recB(v):
                with nc.allow_low_precision(reason="bf16 softmax-recip broadcast"):
                    return v.reciprocal(s_rrow[64:65, 512:1024], s_pvout[1][64:65, :])
            trecB = p.op("vector", _recB, sem="dve")
            epi = (trecA, trecB, js)
        flush_epi()
        t_apB = list(apB_prev)
        kvq_free = ("pe", p.counts["pe"])
        wo_state["last_tat"] = last_tatB

    t_sig_last = None
    rt_bank_free = [pvgate[0], pvgate[1]]
    for t in range(NT_OWN):
        bank = t % 2
        p.wait("tensor", t_rw)
        p.wait("tensor", rt_bank_free[bank])
        mm = None
        for c in range(NCH):
            mm = p.op("tensor", lambda tt, t=t, c=c, bank=bank: tt.matmul(
                PB[0:128, 512 * bank:512 * bank + E],
                xn_own[c][:, 128 * t:128 * (t + 1)], s_rw[:, E * c:E * (c + 1)],
                start=(c == 0), stop=(c == NCH - 1)), sem="pe" if c == NCH - 1 else None)
        p.wait("scalar", mm)
        p.wait("scalar", t_sig_last)
        t_sig_last = p.op("scalar", lambda s, t=t, bank=bank: s.activation(
            out=s_scores[:, E * t:E * (t + 1)], in_=PB[0:128, 512 * bank:512 * bank + E],
            func=AF.Sigmoid), sem="act")
        rt_bank_free[bank] = t_sig_last
    T["rt_done"] = t_sig_last

    wo_state["last_tat"] = last_tatB
    emit_wo(NPAIR - 1, apB_prev)
    T["attn_done"] = oacc_prev
    t_attn_pe = ("pe", p.counts["pe"])


    p.wait("sync", t_attn_pe)
    t_sel = p.op("sync", lambda s: s.dma_start(
        out=s_sel[:].rearrange("k (e c) -> k e c", e=E), in_=sel_d), sem="dsel", dma=True)

    def stage(fns):
        tok = None
        for i, f in enumerate(fns):
            tok = p.op("vector", f, sem="dve" if i == len(fns) - 1 else None)
        return tok

    p.wait("vector", t_sig_last)
    tok = stage([(lambda v, t=t: v.tensor_tensor(
        out=s_sfc[:, E * t:E * (t + 1)], in0=s_scores[:, E * t:E * (t + 1)],
        in1=s_rb_bc[:], op=ALU.add)) for t in range(NT_OWN)])
    p.wait("vector", tok)
    tok = stage([(lambda v, t=t, g=g: v.reduce_max(
        s_gm1[:, 4 * t + g:4 * t + g + 1],
        s_sfc[:, E * t + 8 * g:E * t + 8 * (g + 1)], axis=AX.X))
        for t in range(NT_OWN) for g in range(4)])
    p.wait("vector", tok)
    tok = stage([(lambda v, t=t, g=g: v.tensor_scalar(
        out=s_ism[:, E * t + 8 * g:E * t + 8 * (g + 1)],
        in0=s_sfc[:, E * t + 8 * g:E * t + 8 * (g + 1)],
        scalar1=s_gm1[:, 4 * t + g:4 * t + g + 1], scalar2=None, op0=ALU.is_ge))
        for t in range(NT_OWN) for g in range(4)])
    p.wait("vector", tok)
    tok = stage([(lambda v, t=t, g=g: v.scalar_tensor_tensor(
        out=s_tmp[:, E * t + 8 * g:E * t + 8 * (g + 1)],
        in0=s_ism[:, E * t + 8 * g:E * t + 8 * (g + 1)], scalar=NEG,
        in1=s_sfc[:, E * t + 8 * g:E * t + 8 * (g + 1)], op0=ALU.mult, op1=ALU.add))
        for t in range(NT_OWN) for g in range(4)])
    p.wait("vector", tok)
    tok = stage([(lambda v, t=t, g=g: v.reduce_max(
        s_gm2[:, 4 * t + g:4 * t + g + 1],
        s_tmp[:, E * t + 8 * g:E * t + 8 * (g + 1)], axis=AX.X))
        for t in range(NT_OWN) for g in range(4)])
    p.wait("vector", tok)
    tok = stage([(lambda v, t=t: v.tensor_tensor(
        out=s_gs[:, 4 * t:4 * (t + 1)], in0=s_gm1[:, 4 * t:4 * (t + 1)],
        in1=s_gm2[:, 4 * t:4 * (t + 1)], op=ALU.add)) for t in range(NT_OWN)])
    p.wait("vector", tok)
    tok = stage([(lambda v, t=t: v.reduce_max(
        s_gred[:, t:t + 1], s_gs[:, 4 * t:4 * (t + 1)], axis=AX.X))
        for t in range(NT_OWN)])
    p.wait("vector", tok)
    tok = stage([(lambda v, t=t: v.tensor_scalar(
        out=s_gm1[:, 4 * t:4 * (t + 1)], in0=s_gs[:, 4 * t:4 * (t + 1)],
        scalar1=s_gred[:, t:t + 1], scalar2=None, op0=ALU.is_ge))
        for t in range(NT_OWN)])
    p.wait("vector", tok)
    tok = stage([(lambda v, t=t: v.scalar_tensor_tensor(
        out=s_gm2[:, 4 * t:4 * (t + 1)], in0=s_gm1[:, 4 * t:4 * (t + 1)], scalar=NEG,
        in1=s_gs[:, 4 * t:4 * (t + 1)], op0=ALU.mult, op1=ALU.add))
        for t in range(NT_OWN)])
    p.wait("vector", tok)
    tok = stage([(lambda v, t=t: v.reduce_max(
        s_gred[:, t:t + 1], s_gm2[:, 4 * t:4 * (t + 1)], axis=AX.X))
        for t in range(NT_OWN)])
    p.wait("vector", tok)
    tok = stage([(lambda v, t=t: v.tensor_scalar(
        out=s_gm1[:, 4 * t:4 * (t + 1)], in0=s_gs[:, 4 * t:4 * (t + 1)],
        scalar1=s_gred[:, t:t + 1], scalar2=None, op0=ALU.is_ge))
        for t in range(NT_OWN)])
    p.wait("vector", tok)
    tok = stage([(lambda v, t=t, g=g: v.tensor_scalar(
        out=s_tmp[:, E * t + 8 * g:E * t + 8 * (g + 1)],
        in0=s_sfc[:, E * t + 8 * g:E * t + 8 * (g + 1)],
        scalar1=s_gm1[:, 4 * t + g:4 * t + g + 1], scalar2=None, op0=ALU.mult))
        for t in range(NT_OWN) for g in range(4)])
    p.wait("vector", tok)
    tok = stage([(lambda v, t=t: v.tensor_tensor(
        out=s_tmp[:, E * t:E * (t + 1)], in0=s_tmp[:, E * t:E * (t + 1)],
        in1=s_tilt[:], op=ALU.add)) for t in range(NT_OWN)])
    for it in range(4):
        p.wait("vector", tok)
        tok = stage([(lambda v, t=t: v.reduce_max(
            s_topm[:, t:t + 1], s_tmp[:, E * t:E * (t + 1)], axis=AX.X))
            for t in range(NT_OWN)])
        p.wait("vector", tok)
        tok = stage([(lambda v, t=t: v.tensor_scalar(
            out=s_ism[:, E * t:E * (t + 1)], in0=s_tmp[:, E * t:E * (t + 1)],
            scalar1=s_topm[:, t:t + 1], scalar2=None, op0=ALU.is_ge))
            for t in range(NT_OWN)])
        p.wait("vector", tok)
        if it == 0:
            tok = stage([(lambda v, t=t: v.tensor_copy(
                out=s_selm[:, E * t:E * (t + 1)], in_=s_ism[:, E * t:E * (t + 1)]))
                for t in range(NT_OWN)])
        else:
            tok = stage([(lambda v, t=t: v.tensor_tensor(
                out=s_selm[:, E * t:E * (t + 1)], in0=s_selm[:, E * t:E * (t + 1)],
                in1=s_ism[:, E * t:E * (t + 1)], op=ALU.add)) for t in range(NT_OWN)])
        if it < 3:
            p.wait("vector", tok)
            tok = stage([(lambda v, t=t: v.scalar_tensor_tensor(
                out=s_tmp[:, E * t:E * (t + 1)], in0=s_ism[:, E * t:E * (t + 1)],
                scalar=NEG, in1=s_tmp[:, E * t:E * (t + 1)],
                op0=ALU.mult, op1=ALU.add)) for t in range(NT_OWN)])
    p.wait("vector", tok)
    tok = stage([(lambda v, t=t: v.tensor_tensor(
        out=s_wtm[:, E * t:E * (t + 1)], in0=s_scores[:, E * t:E * (t + 1)],
        in1=s_selm[:, E * t:E * (t + 1)], op=ALU.mult)) for t in range(NT_OWN)])
    p.wait("vector", tok)
    tok = stage([(lambda v, t=t: v.reduce_sum(
        s_den[:, t:t + 1], s_wtm[:, E * t:E * (t + 1)], axis=AX.X))
        for t in range(NT_OWN)])
    p.wait("vector", tok)
    tok = p.op("vector", lambda v: v.tensor_scalar(out=s_den[:], in0=s_den[:],
                                                   scalar1=1e-20, scalar2=None,
                                                   op0=ALU.add), sem="dve")
    p.wait("vector", tok)
    tok = p.op("vector", lambda v: v.reciprocal(s_den[:], s_den[:]), sem="dve")
    p.wait("vector", tok)
    t_w_ready = stage([(lambda v, t=t: v.tensor_scalar(
        out=s_wtm[:, E * t:E * (t + 1)], in0=s_wtm[:, E * t:E * (t + 1)],
        scalar1=s_den[:, t:t + 1], scalar2=2.5, op0=ALU.mult, op1=ALU.mult))
        for t in range(NT_OWN)])
    t_wfm = None
    wtr_free = None
    for t in range(NT_OWN):
        p.wait("tensor", t_w_ready)
        p.wait("tensor", wtr_free)
        p.wait("tensor", sband_free[1])
        ttr = p.op("tensor", lambda tt, t=t: tt.transpose(
            PA[0:E, 1024:1152], s_wtm[:, E * t:E * (t + 1)], s_identf[:]), sem="pe")
        p.wait("scalar", ttr)
        t_wfm = p.op("scalar", lambda s, t=t: s.copy(
            out=s_wfm[:, 128 * t:128 * (t + 1)], in_=PA[0:E, 1024:1152]), sem="act")
        wtr_free = t_wfm
    T["router_done"] = t_wfm
    t_router_pe = ("pe", p.counts["pe"])

    moe_w_free = [None, None]
    t_gate_dma = [None, None]
    t_up_dma = [None, None]
    t_down_dma = [None, None]
    be_bank_free = [T["rt_done"], T["rt_done"]]
    gs_free = [None, None]
    us_free = [None, None]
    hv_free = [None] * NI
    oacc_prev = T["attn_done"]
    dn_bank_free = [None, None]
    gu_bank_free = [sband_free[0], sband_free[0], t_wfm, sband_free[1]]

    def emit_be(e_b):
        bslot = e_b % 2
        t_be = None
        for sl in range(NSL):
            bank = sl % 2
            p.wait("tensor", T["router_done"])
            p.wait("tensor", t_sel)
            p.wait("tensor", be_bank_free[bank])
            p.wait("tensor", dn_bank_free[bank])
            tm = p.op("tensor", lambda t, e_b=e_b, sl=sl, bank=bank: t.matmul(
                PB[:, 512 * bank:512 * (bank + 1)], s_sel[:, 128 * e_b:128 * (e_b + 1)],
                s_wfm[:, 512 * sl:512 * (sl + 1)], start=True, stop=True), sem="pe")
            p.wait("scalar", tm)
            p.wait("scalar", be_slot_free[bslot] if sl == 0 else None)
            t_be = p.op("scalar", lambda s, sl=sl, bank=bank, bslot=bslot: s.copy(
                out=s_be[bslot][:, 512 * sl:512 * (sl + 1)],
                in_=PB[:, 512 * bank:512 * (bank + 1)]), sem="act")
            be_bank_free[bank] = t_be
        return t_be

    t_be_slot = [None, None]
    be_slot_free = [None, None]
    for e in range(E):
        ws = e % 2
        p.wait("gpsimd", moe_w_free[ws])
        if e < 2:
            p.wait("gpsimd", T["kvq_last"] if TOWN >= 2048 else t_attn_pe)
        p.op("gpsimd", lambda g, e=e, ws=ws: g.dma_start(
            out=s_gate[ws][:].rearrange("p (c i) -> p c i", c=NCH),
            in_=gate_d[e].rearrange("(c p) i -> p c i", p=128)), sem=f"dm{ws}", dma=True)
        p.op("gpsimd", lambda g, e=e, ws=ws: g.dma_start(
            out=s_up[ws][:].rearrange("p (c i) -> p c i", c=NCH),
            in_=up_d[e].rearrange("(c p) i -> p c i", p=128)), sem=f"dm{ws}", dma=True)
        if e == 1 or (e == 0 and TOWN < 2048):
            p.wait("gpsimd", t_attn_pe)
        tw3 = p.op("gpsimd", lambda g, e=e, ws=ws: g.dma_start(
            out=s_down[ws][:].rearrange("p (c h) -> p c h", c=NI),
            in_=down_d[e].rearrange("(c p) h -> p c h", p=128)), sem=f"dm{ws}", dma=True)
        t_gate_dma[ws] = t_up_dma[ws] = t_down_dma[ws] = tw3

        if e == 0:
            t_be_slot[0] = emit_be(0)
        t_be = t_be_slot[e % 2]
        ta = None
        for i in range(NI):
            for sl2 in range(NSL):
                col0 = 512 * sl2
                gb = sl2 % 2
                ub = 2 + sl2 % 2
                ring = (i * NSL + sl2) % 2
                p.wait("tensor", t_gate_dma[ws])
                p.wait("tensor", gu_bank_free[gb])
                mm = None
                for c in range(NCH):
                    mm = p.op("tensor", lambda t, ws=ws, c=c, i=i, col0=col0, gb=gb: t.matmul(
                        PA[:, 512 * gb:512 * (gb + 1)],
                        s_gate[ws][:, IMOE * c + 128 * i:IMOE * c + 128 * (i + 1)],
                        xn_own[c][:, col0:col0 + 512],
                        start=(c == 0), stop=(c == NCH - 1)),
                        sem="pe" if c == NCH - 1 else None)
                p.wait("scalar", mm)
                p.wait("scalar", gs_free[ring])
                tsil = p.op("scalar", lambda s, gb=gb, ring=ring: s.activation(
                    out=s_gsb[ring][:], in_=PA[:, 512 * gb:512 * (gb + 1)],
                    func=AF.Silu), sem="act")
                gu_bank_free[gb] = tsil
                p.wait("tensor", t_up_dma[ws])
                p.wait("tensor", gu_bank_free[ub])
                mm = None
                for c in range(NCH):
                    mm = p.op("tensor", lambda t, ws=ws, c=c, i=i, col0=col0, ub=ub: t.matmul(
                        PA[:, 512 * ub:512 * (ub + 1)],
                        s_up[ws][:, IMOE * c + 128 * i:IMOE * c + 128 * (i + 1)],
                        xn_own[c][:, col0:col0 + 512],
                        start=(c == 0), stop=(c == NCH - 1)),
                        sem="pe" if c == NCH - 1 else None)
                p.wait("vector", mm)
                p.wait("vector", t_be)
                p.wait("vector", us_free[ring])
                tus = p.op("vector", lambda v, ub=ub, ring=ring, col0=col0, e=e: v.scalar_tensor_tensor(
                    out=s_usb[ring][:], in0=PA[:, 512 * ub:512 * (ub + 1)], scalar=1.0,
                    in1=s_be[e % 2][:, col0:col0 + 512], op0=ALU.mult, op1=ALU.mult),
                    sem="dve")
                gu_bank_free[ub] = tus
                p.wait("vector", tus)
                p.wait("vector", tsil)
                p.wait("vector", hv_free[i])
                thv = p.op("vector", lambda v, i=i, ring=ring, col0=col0: v.tensor_tensor(
                    out=s_hv[i][:, col0:col0 + 512], in0=s_gsb[ring][:],
                    in1=s_usb[ring][:], op=ALU.mult), sem="dve")
                gs_free[ring] = thv
                us_free[ring] = thv
                hv_free[i] = thv
        if e == E - 2:
            T["gu30"] = ("pe", p.counts["pe"])
        if e == E - 1:
            T["gu31"] = ("pe", p.counts["pe"])
        be_slot_free[e % 2] = ("dve", p.counts["dve"])
        if e + 1 < E:
            t_be_slot[(e + 1) % 2] = emit_be(e + 1)
        t_benext = t_be_slot[(e + 1) % 2] if e + 1 < E else None
        t_hv_all = list(hv_free)
        for oc in range(NCH):
            for sl2 in range(NSL):
                col0 = 512 * sl2
                bank = (oc * NSL + sl2) % 2
                p.wait("tensor", t_down_dma[ws])
                p.wait("tensor", t_benext)
                p.wait("tensor", dn_bank_free[bank])
                p.wait("tensor", t_hv_all)
                mm = None
                for i in range(NI):
                    mm = p.op("tensor", lambda t, ws=ws, i=i, oc=oc, col0=col0, bank=bank: t.matmul(
                        PB[:, 512 * bank:512 * (bank + 1)],
                        s_down[ws][:, DM * i + 128 * oc:DM * i + 128 * (oc + 1)],
                        s_hv[i][:, col0:col0 + 512],
                        start=(i == 0), stop=(i == NI - 1)),
                        sem="pe" if i == NI - 1 else None)
                p.wait("vector", mm)
                p.wait("vector", oacc_prev)
                ta = p.op("vector", lambda v, oc=oc, col0=col0, bank=bank: v.scalar_tensor_tensor(
                    out=o_acc[oc][:, col0:col0 + 512],
                    in0=PB[:, 512 * bank:512 * (bank + 1)], scalar=1.0,
                    in1=o_acc[oc][:, col0:col0 + 512], op0=ALU.mult, op1=ALU.add),
                    sem="dve")
                dn_bank_free[bank] = ta
        hv_done = ("pe", p.counts["pe"])
        for i in range(NI):
            hv_free[i] = hv_done
        oacc_prev = ta
        moe_w_free[ws] = ("pe", p.counts["pe"])
        if e == E - 2:
            T["e30_done"] = moe_w_free[ws]
        if e == E - 1:
            T["e31_done"] = moe_w_free[ws]
    T["moe_done"] = oacc_prev
    t_moe_pe = ("pe", p.counts["pe"])

    big = TOWN >= 2048
    p.wait("gpsimd", T["gu30"] if big else t_moe_pe)
    t_sg1 = p.op("gpsimd", lambda g: g.dma_start(
        out=s_sg1[:].rearrange("p (c i) -> p c i", c=NCH),
        in_=sg_d.rearrange("(c p) i -> p c i", p=128)[:, :, 0:512]), sem="dsh", dma=True)
    p.wait("gpsimd", T["gu31"] if big else None)
    t_sg2 = p.op("gpsimd", lambda g: g.dma_start(
        out=s_sg2[:].rearrange("p (c i) -> p c i", c=NCH),
        in_=sg_d.rearrange("(c p) i -> p c i", p=128)[:, :, 512:1024]), sem="dsh", dma=True)
    t_su1 = p.op("gpsimd", lambda g: g.dma_start(
        out=s_su1[:].rearrange("p (c i) -> p c i", c=NCH),
        in_=su_d.rearrange("(c p) i -> p c i", p=128)[:, :, 0:512]), sem="dsh", dma=True)
    t_su2 = p.op("gpsimd", lambda g: g.dma_start(
        out=s_su2[:].rearrange("p (c i) -> p c i", c=NCH),
        in_=su_d.rearrange("(c p) i -> p c i", p=128)[:, :, 512:1024]), sem="dsh", dma=True)
    p.wait("gpsimd", T["e30_done"] if big else None)
    t_sd1 = p.op("gpsimd", lambda g: g.dma_start(
        out=s_sd1[:].rearrange("p (c h) -> p c h", c=NSH),
        in_=sd_d.rearrange("(c p) h -> p c h", p=128)[:, :, 0:512]), sem="dsh", dma=True)
    p.wait("gpsimd", T["e31_done"] if big else None)
    t_sd2 = p.op("gpsimd", lambda g: g.dma_start(
        out=s_sd2[:].rearrange("p (c h) -> p c h", c=NSH),
        in_=sd_d.rearrange("(c p) h -> p c h", p=128)[:, :, 512:1024]), sem="dsh", dma=True)

    t_shw = ("dsh", p.counts["dsh"])
    t_sg1 = t_sg2 = t_su1 = t_su2 = t_sd1 = t_sd2 = t_shw
    gssh_free = [None, None]
    hvsh_free = [None] * NSH
    oacc_prev_sh = T["moe_done"]
    for th in range(2):
        for i in range(NSH):
            gw = s_sg1 if i < 4 else s_sg2
            uw = s_su1 if i < 4 else s_su2
            tgw = t_sg1 if i < 4 else t_sg2
            tuw = t_su1 if i < 4 else t_su2
            ii = i % 4
            for sl2 in range(NSLH):
                col0 = TH * th + 512 * sl2
                gb = sl2 % 2
                ub = 2 + sl2 % 2
                ring = (i + sl2) % 2
                p.wait("tensor", tgw)
                p.wait("tensor", gu_bank_free[gb])
                mm = None
                for c in range(NCH):
                    mm = p.op("tensor", lambda t, c=c, ii=ii, col0=col0, gb=gb, gw=gw: t.matmul(
                        PA[:, 512 * gb:512 * (gb + 1)],
                        gw[:, 512 * c + 128 * ii:512 * c + 128 * (ii + 1)],
                        xn_own[c][:, col0:col0 + 512],
                        start=(c == 0), stop=(c == NCH - 1)),
                        sem="pe" if c == NCH - 1 else None)
                p.wait("scalar", mm)
                p.wait("scalar", gssh_free[ring])
                tsil = p.op("scalar", lambda s, gb=gb, ring=ring: s.activation(
                    out=s_gssh[ring][:], in_=PA[:, 512 * gb:512 * (gb + 1)],
                    func=AF.Silu), sem="act")
                gu_bank_free[gb] = tsil
                p.wait("tensor", tuw)
                p.wait("tensor", gu_bank_free[ub])
                mm = None
                for c in range(NCH):
                    mm = p.op("tensor", lambda t, c=c, ii=ii, col0=col0, ub=ub, uw=uw: t.matmul(
                        PA[:, 512 * ub:512 * (ub + 1)],
                        uw[:, 512 * c + 128 * ii:512 * c + 128 * (ii + 1)],
                        xn_own[c][:, col0:col0 + 512],
                        start=(c == 0), stop=(c == NCH - 1)),
                        sem="pe" if c == NCH - 1 else None)
                p.wait("vector", mm)
                p.wait("vector", tsil)
                p.wait("vector", hvsh_free[i])
                thv = p.op("vector", lambda v, i=i, ub=ub, ring=ring, sl2=sl2: v.scalar_tensor_tensor(
                    out=s_hvsh[i][:, 512 * sl2:512 * (sl2 + 1)],
                    in0=PA[:, 512 * ub:512 * (ub + 1)], scalar=1.0,
                    in1=s_gssh[ring][:], op0=ALU.mult, op1=ALU.mult), sem="dve")
                gu_bank_free[ub] = thv
                gssh_free[ring] = thv
                hvsh_free[i] = thv
        t_hv_half = list(hvsh_free)
        for sl2 in range(NSLH):
            for oc in range(NCH):
                col0 = TH * th + 512 * sl2
                bank = (sl2 * NCH + oc) % 2
                dw_tok = t_sd1 if oc < 4 else t_sd2
                dwbuf = s_sd1 if oc < 4 else s_sd2
                oc2 = oc % 4
                p.wait("tensor", dw_tok)
                p.wait("tensor", dn_bank_free[bank])
                p.wait("tensor", t_hv_half)
                mm = None
                for i in range(NSH):
                    mm = p.op("tensor", lambda t, i=i, oc2=oc2, sl2=sl2, bank=bank, dwbuf=dwbuf: t.matmul(
                        PB[:, 512 * bank:512 * (bank + 1)],
                        dwbuf[:, 512 * i + 128 * oc2:512 * i + 128 * (oc2 + 1)],
                        s_hvsh[i][:, 512 * sl2:512 * (sl2 + 1)],
                        start=(i == 0), stop=(i == NSH - 1)),
                        sem="pe" if i == NSH - 1 else None)
                p.wait("vector", mm)
                p.wait("vector", oacc_prev_sh)
                ta = p.op("vector", lambda v, oc=oc, col0=col0, bank=bank: v.scalar_tensor_tensor(
                    out=o_acc[oc][:, col0:col0 + 512],
                    in0=PB[:, 512 * bank:512 * (bank + 1)], scalar=1.0,
                    in1=o_acc[oc][:, col0:col0 + 512], op0=ALU.mult, op1=ALU.add),
                    sem="dve")
                dn_bank_free[bank] = ta
                if oc == NCH - 1:
                    T.setdefault("sh_done_sl", {})[(th, sl2)] = ta
        hv_done = ("pe", p.counts["pe"])
        for i in range(NSH):
            hvsh_free[i] = hv_done
        oacc_prev_sh = ta
    t_shd = T["shared_done"] = oacc_prev_sh
    sh_done_sl = dict(T["sh_done_sl"])
    for oc in range(NCH):
        for sl2k, tok in list(sh_done_sl.items()):
            pass
    for (th2, sl22), tok in sorted(T["sh_done_sl"].items()):
        col0 = TH * th2 + 512 * sl22
        tb_last = None
        for oc in range(NCH):
            p.wait("vector", tok)
            tb_last = p.op("vector", lambda v, oc=oc, col0=col0: v.tensor_scalar(
                out=o_acc[oc][:, col0:col0 + 512], in0=o_acc[oc][:, col0:col0 + 512],
                scalar1=s_bofm[:, oc:oc + 1], scalar2=None, op0=ALU.add), sem="dve")
        sh_done_sl[(th2, sl22)] = tb_last
    t_sh_pe = ("pe", p.counts["pe"])

    slot_free = [ln_xring_last[0], ln_xring_last[1]]
    or_free = [t_attn_pe, t_attn_pe]
    ptf_free = [None, None]
    nring = 4 if TOWN >= 2048 else 2
    or_free = [t_attn_pe, t_attn_pe, t_sh_pe, t_sh_pe][:nring]
    for t in range(NT_OWN):
        sl = t % 2
        xr = t % 2
        orr = t % nring
        p.wait("sync", slot_free[sl])
        tok_ld = p.op("sync", lambda s, t=t, sl=sl: s.dma_start(
            out=x_ring[sl][:], in_=x_seq[128 * t:128 * (t + 1), :]), sem=f"dx{sl}", dma=True)
        th_t, sl2_t = (t * 128) // TH, ((t * 128) % TH) // 512
        p.wait("tensor", sh_done_sl[(th_t, sl2_t)])
        p.wait("tensor", ptf_free[xr])
        tt = None
        for c in range(NCH):
            tt = p.op("tensor", lambda tr, t=t, c=c, xr=xr: tr.transpose(
                PT[:, 1024 * xr + 128 * c:1024 * xr + 128 * (c + 1)],
                o_acc[c][:, 128 * t:128 * (t + 1)], s_identb[:]),
                sem="pe" if c == NCH - 1 else None)
        p.wait("vector", tt)
        p.wait("vector", tok_ld)
        p.wait("vector", or_free[orr])
        tadd = p.op("vector", lambda v, sl=sl, xr=xr, orr=orr: v.scalar_tensor_tensor(
            out=out_ring[orr][:], in0=PT[:, 1024 * xr:1024 * (xr + 1)], scalar=1.0,
            in1=x_ring[sl][:], op0=ALU.mult, op1=ALU.add), sem="dve")
        ptf_free[xr] = tadd
        slot_free[sl] = tadd
        p.wait("sync", tadd)
        tst = p.op("sync", lambda s, t=t, orr=orr: s.dma_start(
            out=out_d[128 * t:128 * (t + 1), :], in_=out_ring[orr][:]), sem=f"dst{orr}", dma=True)
        or_free[orr] = tst
    for k in range(4):
        if f"dst{k}" in p.counts:
            p.wait("sync", (f"dst{k}", p.counts[f"dst{k}"]))

    p.check()
    p.emit()
    return nc


def _consts(core_parity):
    ident_bf = np.eye(128, dtype=ml_dtypes.bfloat16)
    ident_f = np.eye(128, dtype=np.float32)
    kk, qq = np.meshgrid(np.arange(128), np.arange(128), indexing="ij")
    tri = (kk <= qq).astype(ml_dtypes.bfloat16)
    band_other = np.full((128, 128), 0.0 if core_parity == 0 else 1.0,
                         dtype=ml_dtypes.bfloat16)
    tilt = np.broadcast_to((31 - np.arange(32)).astype(np.float32) * TILT,
                           (128, 32)).copy()
    sel = np.zeros((32, 32, 128), dtype=ml_dtypes.bfloat16)
    for e in range(32):
        sel[e, e, :] = 1.0
    ones = np.ones((128, 128), dtype=np.float32)
    return dict(ident_bf=ident_bf, ident_f=ident_f, tri_bf=tri,
                band_other=band_other, tilt=tilt, selmat=sel, ones_f=ones)


_NC_CACHE = {}


def _get_nc(seq, heads):
    key = (seq, heads)
    if key not in _NC_CACHE:
        _NC_CACHE[key] = build_nc(dict(seq=seq, heads=heads))
    return _NC_CACHE[key]


def make_in_maps(x, wq, wk, wv, wo, bo, rw, rb, gate, up, down, sg, su, sd,
                 n_cores):
    B, S, DM = x.shape
    f32 = np.float32
    base = dict(
        wq=np.ascontiguousarray(wq, f32), wk=np.ascontiguousarray(wk, f32),
        wv=np.ascontiguousarray(wv, f32), wo=np.ascontiguousarray(wo, f32),
        bo=np.ascontiguousarray(np.asarray(bo, f32).reshape(1, DM)),
        bo_fm=np.ascontiguousarray(np.asarray(bo, f32).reshape(8, 128).T),
        rwT=np.ascontiguousarray(np.asarray(rw, f32).T),
        rbias=np.ascontiguousarray(np.asarray(rb, f32).reshape(1, 32)),
        gateT=np.ascontiguousarray(np.asarray(gate, f32).transpose(0, 2, 1)),
        upT=np.ascontiguousarray(np.asarray(up, f32).transpose(0, 2, 1)),
        downT=np.ascontiguousarray(np.asarray(down, f32).transpose(0, 2, 1)),
        sgT=np.ascontiguousarray(np.asarray(sg, f32).T),
        suT=np.ascontiguousarray(np.asarray(su, f32).T),
        sdT=np.ascontiguousarray(np.asarray(sd, f32).T),
    )
    in_maps = []
    for core in range(n_cores):
        n, j = core // 2, core % 2
        xt = np.asarray(x[n], f32).reshape(S // 128, 128, DM)
        x_seq = np.ascontiguousarray(
            np.concatenate([xt[j::2], xt[1 - j::2]], axis=0).reshape(S, DM))
        m = dict(base)
        m["x_seq"] = x_seq
        m.update(_consts(j))
        in_maps.append(m)
    return in_maps


def assemble(results, B, S, DM):
    out = np.empty((B, S, DM), dtype=np.float32)
    for core in range(2 * B):
        n, j = core // 2, core % 2
        o = np.asarray(results[core]["out"], dtype=np.float32)
        out[n].reshape(S // 128, 128, DM)[j::2] = o.reshape(S // 256, 128, DM)
    return out


def kernel(L0_attn_x, L0_attn_Wq, L0_attn_Wk, L0_attn_Wv, L0_attn_Wo, L0_attn_bo,
           L0_ffn_router_weight, L0_ffn_router_bias, L0_ffn_gate_w, L0_ffn_up_w,
           L0_ffn_down_w, L0_ffn_shared_gate_w, L0_ffn_shared_up_w,
           L0_ffn_shared_down_w):
    x = np.ascontiguousarray(np.asarray(L0_attn_x, dtype=np.float32))
    B, S, DM = x.shape
    nc = _get_nc(S, 16)
    in_maps = make_in_maps(x, L0_attn_Wq, L0_attn_Wk, L0_attn_Wv, L0_attn_Wo,
                           L0_attn_bo, L0_ffn_router_weight, L0_ffn_router_bias,
                           L0_ffn_gate_w, L0_ffn_up_w, L0_ffn_down_w,
                           L0_ffn_shared_gate_w, L0_ffn_shared_up_w,
                           L0_ffn_shared_down_w, 2 * B)
    res = run_bass_kernel_spmd(nc, in_maps, core_ids=list(range(2 * B)))
    return assemble(res.results, B, S, DM)




